# version 16 (final)
# baseline (speedup 1.0000x reference)
import numpy as np
import ml_dtypes
import concourse.bass as bass
import concourse.mybir as mybir
from concourse.bass_utils import run_bass_kernel_spmd

F32 = mybir.dt.float32
BF16 = mybir.dt.bfloat16
AF = mybir.ActivationFunctionType
ALU = mybir.AluOpType
AX = mybir.AxisListType.X

T = 2048
D = 1024
NT = 16
NPANEL = 4
PT = 4
DEPTH = 4
HEADS = 8
DH = 128
NLOC = 4
WINDOW = 128
BUCKET = 64
FF = 4096
EPS = 1e-5
NEG = -1e30


class Chan:
    def __init__(self, sem, step=1):
        self.sem = sem
        self.step = step
        self.count = 0

    def at(self):
        return self.count


class DmaChan:

    R = 8

    def __init__(self, subs):
        self.subs = subs
        self.k = 0

    def issue_waits(self):
        sl = self.k % self.R
        base = 16 * (self.k // self.R)
        return [(self.subs[sl], base)] if base > 0 else []

    def advance(self):
        sl = self.k % self.R
        self.subs[sl].count += 16
        self.k += 1
        return self.subs[sl]

    def at(self):
        return [(sub, sub.count) for sub in self.subs if sub.count > 0]


class Builder:
    ENGS = ("tensor", "vector", "scalar", "gpsimd", "sync")

    def __init__(self, nc):
        self.nc = nc
        self.q = {e: [] for e in self.ENGS}
        self.meta = {e: [] for e in self.ENGS}
        self.waited = {e: {} for e in self.ENGS}

    def emit(self, eng, fn, waits=(), inc=None, inc_n=None, tag=None):
        if isinstance(inc, DmaChan):
            waits = list(waits) + inc.issue_waits()
            sub = inc.subs[inc.k % inc.R]
            inc.k += 1
            inc = sub
            inc_n = 16
        w = []
        seen = self.waited[eng]
        flat = []
        for item in waits:
            if item is None:
                continue
            if isinstance(item, tuple) and len(item) == 2 and isinstance(item[0], Chan):
                flat.append(item)
            else:
                for sub in item:
                    if sub is not None:
                        flat.append(sub)
        w_chans = []
        for ch, v in flat:
            if v <= 0 or seen.get(ch, 0) >= v:
                continue
            seen[ch] = v
            w.append((ch.sem, v))
            w_chans.append((ch, v))
        n = None
        if inc is not None:
            n = inc_n if inc_n is not None else inc.step
            inc.count += n
        w = tuple(w)

        def run(e, fn=fn, w=w, inc=inc, n=n):
            for sem, v in w:
                e.wait_ge(sem, v)
            ins = fn(e)
            if inc is not None:
                ins.then_inc(inc.sem, n)
        self.q[eng].append(run)
        self.meta[eng].append((tuple(w_chans), (inc, n)))
        if tag is not None:
            self.tags = getattr(self, "tags", [])
            self.tags.append((tag, eng, tuple((id(ch), v) for ch, v in w_chans),
                              (id(inc) if inc else None, n)))
        return inc.count if inc is not None else None

    def check(self):
        pos = {e: 0 for e in self.ENGS}
        counts = {}
        progress = True
        total = sum(len(self.meta[e]) for e in self.ENGS)
        done = 0
        while progress:
            progress = False
            for e in self.ENGS:
                while pos[e] < len(self.meta[e]):
                    waits, (inc, n) = self.meta[e][pos[e]]
                    if any(counts.get(id(ch), 0) < v for ch, v in waits):
                        break
                    if inc is not None:
                        counts[id(inc)] = counts.get(id(inc), 0) + n
                    pos[e] += 1
                    done += 1
                    progress = True
        if done != total:
            detail = {e: (pos[e], len(self.meta[e])) for e in self.ENGS}
            stuck = {}
            for e in self.ENGS:
                if pos[e] < len(self.meta[e]):
                    waits, _ = self.meta[e][pos[e]]
                    stuck[e] = [(getattr(ch.sem, "name", "?"), v, counts.get(id(ch), 0))
                                for ch, v in waits]
            raise RuntimeError(f"DEADLOCK: progress {detail}; stuck on {stuck}")
        return counts

    def finalize(self):
        self.check()
        with self.nc.Block() as blk:
            for eng in self.ENGS:
                if not self.q[eng]:
                    continue

                def runner(e, eng=eng):
                    for f in self.q[eng]:
                        f(e)
                getattr(blk, eng)(runner)


def build_program(taps=(), depth=DEPTH, stop_after=None):
    nc = bass.Bass("TRN2", target_bir_lowering=False)
    b = Builder(nc)
    from contextlib import ExitStack
    es = ExitStack()
    nc._keepalive_es = es

    def dram(name, shape, dtype, kind="ExternalInput"):
        return nc.dram_tensor(name, shape, dtype, kind=kind).ap()

    def internal(name, shape, dtype, tapname=None):
        kind = "ExternalOutput" if (tapname in taps) else "Internal"
        return nc.dram_tensor(name, shape, dtype, kind=kind).ap()

    sbuf = lambda name, shape, dtype: nc.alloc_sbuf_tensor(name, shape, dtype)
    psum = lambda name, shape, dtype=F32: nc.alloc_psum_tensor(name, shape, dtype)

    def chan(name, step=1):
        return Chan(es.enter_context(nc.semaphore(name)), step)

    x_h = dram("x", [T, D], F32)
    Wq_h = dram("Wq", [DEPTH, D, D], F32)
    Wk_h = dram("Wk", [DEPTH, D, D], F32)
    Wv_h = dram("Wv", [DEPTH, D, D], F32)
    Wo_h = dram("Wo", [DEPTH, D, D], F32)
    bo_h = dram("bo", [DEPTH, D], F32)
    g1_h = dram("ln1_g", [DEPTH, D], F32)
    b1_h = dram("ln1_b", [DEPTH, D], F32)
    W1_h = dram("W1", [DEPTH, D, FF], F32)
    bb1_h = dram("b1", [DEPTH, FF], F32)
    W2_h = dram("W2", [DEPTH, FF, D], F32)
    bb2_h = dram("b2", [DEPTH, D], F32)
    g2_h = dram("ln2_g", [DEPTH, D], F32)
    b2ln_h = dram("ln2_b", [DEPTH, D], F32)
    gate_w_h = dram("gate_w", [D, 4], F32)
    gate_b_h = dram("gate_b", [1, 4], F32)
    sw1_h = dram("sw1", [2, D, FF], F32)
    sb1_h = dram("sb1", [2, FF], F32)
    sw2_h = dram("sw2", [2, FF, D], F32)
    sb2_h = dram("sb2", [2, D], F32)
    rw1_h = dram("rw1", [4, D, FF], F32)
    rb1_h = dram("rb1", [4, FF], F32)
    rw2_h = dram("rw2", [4, FF, D], F32)
    rb2_h = dram("rb2", [4, D], F32)
    ident_h = dram("ident", [128, 128], BF16)
    masks_h = dram("masks", [128, 512], BF16)
    out_h = dram("out", [T, D], F32, kind="ExternalOutput")

    a_hbm = internal("a_scratch", [T, D], F32, "a")
    xn_hbm = internal("xn_scratch", [T, D], BF16, "xnT")
    h2T_hbm = internal("h2T_scratch", [128, 8, T], BF16, "h2T")
    acc_hbm = internal("acc_scratch", [T, D], F32, "acc")
    dbg_h1T = internal("dbg_h1T", [128, 8, T], BF16, "h1T") if "h1T" in taps else None
    if "qkv" in taps:
        dbg_q = dram("dbg_q", [128, 2048], BF16, kind="ExternalOutput")
        dbg_k = dram("dbg_k", [128, 4 * T], BF16, kind="ExternalOutput")
        dbg_ql = dram("dbg_ql", [128, 2048], BF16, kind="ExternalOutput")
        dbg_kl = dram("dbg_kl", [128, 4 * 512], BF16, kind="ExternalOutput")
        dbg_v = dram("dbg_v", [128, 4 * D], BF16, kind="ExternalOutput")
    else:
        dbg_q = dbg_k = dbg_ql = dbg_kl = dbg_v = None
    dbg_ao = internal("dbg_ao", [128, 8, T], BF16, "ao") if "ao" in taps else None
    dbg_att = dram("dbg_att", [128, 1024], F32, kind="ExternalOutput") if "att" in taps else None
    dbg_lin = dram("dbg_lin", [128, 1024], F32, kind="ExternalOutput") if "lin" in taps else None

    at = [sbuf(f"at{i}", [128, D], F32) for i in range(PT)]
    st6 = [sbuf(f"st6_{i}", [128, 12], F32) for i in range(PT)]
    st2 = [sbuf(f"st2_{i}", [128, 2], F32) for i in range(PT)]
    rstd = [sbuf(f"rstd{i}", [128, 1], F32) for i in range(PT)]
    tinyA = sbuf("tinyA", [128, 4], F32)
    st6w = [sbuf(f"st6w_{i}", [128, 12], F32) for i in range(PT)]
    st2w = [sbuf(f"st2w_{i}", [128, 2], F32) for i in range(PT)]
    rstdw = [sbuf(f"rstdw{i}", [128, 1], F32) for i in range(PT)]
    tinyW = sbuf("tinyW", [128, 4], F32)
    hraw = [sbuf(f"hraw{i}", [128, D], BF16) for i in range(PT)]
    h1T = sbuf("h1T", [128, 8 * 512], BF16)
    aoT = sbuf("aoT", [128, 8 * 512], BF16)
    qDloc = sbuf("qDloc", [128, 4 * 512], BF16)
    kDloc = sbuf("kDloc", [128, 4 * T], BF16)
    qDlin = sbuf("qDlin", [128, 4 * 512], BF16)
    klin = [sbuf(f"klin{i}", [128, 512], BF16) for i in range(PT)]
    NVSLOT = 9
    vtok = [sbuf(f"vtok{i}", [128, D], BF16) for i in range(NVSLOT)]
    ctx_state = sbuf("ctx_state", [128, 4 * 128], F32)
    krun = sbuf("krun", [128, 4], F32)
    recips = sbuf("recips", [128, 8], F32)
    sm_e = [sbuf(f"sm_e{i}", [128, 512], F32) for i in range(2)]
    sm_m = sbuf("sm_m", [128, 32], F32)
    sm_s = sbuf("sm_s", [128, 32], F32)
    sm_r = sbuf("sm_r", [128, 32], F32)
    katT = [sbuf(f"katT{i}", [128, 256], BF16) for i in range(4)]
    wres = [sbuf(f"wres{i}", [128, 512], F32) for i in range(2)]
    chat = [sbuf(f"chat{i}", [128, 128], BF16) for i in range(2)]
    krec_tmp = sbuf("krec_tmp", [128, 4], F32)
    qsm = [sbuf(f"qsm{i}", [128, 512], BF16) for i in range(2)]
    h2b = [sbuf(f"h2b{i}", [128, 8 * 256], BF16) for i in range(2)]
    gw_sb = sbuf("gw_sb", [128, 32], BF16)
    gb_row = sbuf("gb_row", [1, 4], BF16)
    gwt = sbuf("gwt", [128, 64], F32)
    gsc = sbuf("gsc", [128, 8], F32)
    scg = sbuf("scg", [128, 64], F32)
    ident = sbuf("ident_sb", [128, 128], BF16)
    masks = sbuf("masks_sb", [128, 512], BF16)
    ones_bf = sbuf("ones_bf", [128, 128], BF16)
    eps_col = sbuf("eps_col", [128, 1], F32)
    g1c = sbuf("g1c", [128, 8], F32)
    b1c = sbuf("b1c", [128, 8], F32)
    g2c = sbuf("g2c", [128, 8], F32)
    b2c = sbuf("b2c", [128, 8], F32)
    grow = sbuf("grow", [1, 4 * D], BF16)
    borow = sbuf("borow", [1, D], BF16)
    b1row = sbuf("b1row", [1, FF], BF16)
    b1cc = sbuf("b1cc", [128, 64], F32)
    b2row = sbuf("b2row", [1, D], BF16)
    ut = sbuf("ut", [128, 16 * 256], BF16)
    AR = 32 * 1024
    arena = sbuf("arena", [128, AR], BF16)

    mmA = psum("mmA", [128, 512], F32)
    mmB = psum("mmB", [128, 512], F32)
    trA = psum("trA", [128, 1024], BF16)
    trB = psum("trB", [128, 1024], BF16)
    dA = psum("dA", [128, 512], F32)
    dB = psum("dB", [128, 512], F32)
    kcum = psum("kcum", [128, 512], F32)
    spare = psum("spare", [128, 512], F32)

    c_pe = chan("c_pe")
    c_dv = chan("c_dv")
    c_ac = chan("c_ac")
    c_ldw = DmaChan([chan(f"c_ldw{i}", 16) for i in range(DmaChan.R)])
    c_ldx = DmaChan([chan(f"c_ldx{i}", 16) for i in range(DmaChan.R)])
    c_st = DmaChan([chan(f"c_st{i}", 16) for i in range(DmaChan.R)])
    c_gp = chan("c_gp")

    E = b.emit

    E("sync", lambda e: e.dma_start(out=ident[:, :], in_=ident_h), inc=c_ldx)
    E("sync", lambda e: e.dma_start(out=masks[:, :], in_=masks_h), inc=c_ldx)
    E("gpsimd", lambda e: e.memset(ones_bf[:, :], 1.0), inc=c_gp)
    E("gpsimd", lambda e: e.memset(eps_col[:, :], EPS), inc=c_gp)
    consts_ldx = c_ldx.at()
    consts_gp = c_gp.at()

    free = {}
    def buf_wait(name):
        return free.get(name, ())
    def buf_set(name, *pairs):
        free[name] = list(pairs)

    bank_free = {id(p): [] for p in (mmA, mmB, trA, trB, dA, dB, kcum, spare)}
    def bfree(p):
        return bank_free[id(p)]
    def bset(p, *pairs):
        bank_free[id(p)] = list(pairs)

    guards = {"rows_pe": 0, "borow_pe": 0, "b2row_pe": 0, "b1cc_ac": 0}

    def load_layer_smalls(L):
        E("gpsimd", lambda e: e.dma_start(out=grow[:, 0:D], in_=g1_h[L:L + 1, :]),
          waits=[(c_pe, guards["rows_pe"])], inc=c_ldw)
        E("gpsimd", lambda e: e.dma_start(out=grow[:, D:2 * D], in_=b1_h[L:L + 1, :]), inc=c_ldw)
        E("gpsimd", lambda e: e.dma_start(out=grow[:, 2 * D:3 * D], in_=g2_h[L:L + 1, :]), inc=c_ldw)
        E("gpsimd", lambda e: e.dma_start(out=grow[:, 3 * D:4 * D], in_=b2ln_h[L:L + 1, :]), inc=c_ldw)
        E("gpsimd", lambda e: e.dma_start(out=borow[:, :], in_=bo_h[L:L + 1, :]),
          waits=[(c_pe, guards["borow_pe"])], inc=c_ldw)
        E("gpsimd", lambda e: e.dma_start(out=b1row[:, :], in_=bb1_h[L:L + 1, :]), inc=c_ldw)
        E("gpsimd", lambda e: e.dma_start(out=b2row[:, :], in_=bb2_h[L:L + 1, :]),
          waits=[(c_pe, guards["b2row_pe"])], inc=c_ldw)
        w = c_ldw.at()

        def tr_rows(e):
            ins = None
            for j in range(4):
                for c in range(8):
                    ins = e.matmul(spare[:, j * 8 + c:j * 8 + c + 1],
                                   grow[0:1, j * D + c * 128:j * D + (c + 1) * 128],
                                   ones_bf[0:1, 0:1],
                                   start=(j == 0 and c == 0), stop=False)
            for c in range(32):
                ins = e.matmul(spare[:, 32 + c:33 + c],
                               b1row[0:1, c * 128:(c + 1) * 128],
                               ones_bf[0:1, 0:1], start=False, stop=(c == 31))
            return ins
        E("tensor", tr_rows, waits=[w, (c_gp, consts_gp)] + list(bfree(spare)), inc=c_pe)
        pe = c_pe.at()
        guards["rows_pe"] = pe
        E("vector", lambda e: e.tensor_copy(g1c[:, :], spare[:, 0:8]), waits=[(c_pe, pe)])
        E("vector", lambda e: e.tensor_copy(b1c[:, :], spare[:, 8:16]))
        E("vector", lambda e: e.tensor_copy(g2c[:, :], spare[:, 16:24]))
        E("vector", lambda e: e.tensor_copy(b2c[:, :], spare[:, 24:32]))
        E("vector", lambda e: e.tensor_copy(b1cc[:, 0:32], spare[:, 32:64]),
          waits=[(c_ac, guards["b1cc_ac"])], inc=c_dv)
        bset(spare, (c_dv, c_dv.at()))
        return c_dv.at()

    st2_act_read = [0] * PT
    rstd_dv_read = [0] * PT

    def tile_norm_stats(specs, extra_waits=()):
        v2s = {}
        for i, src in specs:
            E("vector", lambda e, i=i, src=src: e.bn_stats(st6[i][:, 0:6], src[:, 0:512]),
              waits=extra_waits)
            E("vector", lambda e, i=i, src=src: e.bn_stats(st6[i][:, 6:12], src[:, 512:1024]),
              inc=c_dv)
            v1 = c_dv.at()
            E("vector", lambda e, i=i: e.bn_aggr(st2[i][:, :], st6[i][:, :]),
              waits=[(c_dv, v1), (c_ac, st2_act_read[i])], inc=c_dv)
            v2s[i] = c_dv.at()
        a1s = {}
        for i, _ in specs:
            E("scalar", lambda e, i=i: e.activation(tinyA[:, i:i + 1], st2[i][:, 1:2], AF.Ln,
                                                    bias=eps_col[:, 0:1]),
              waits=[(c_dv, v2s[i]), (c_gp, consts_gp)], inc=c_ac)
            a_ln = c_ac.at()
            E("scalar", lambda e, i=i: e.activation(rstd[i][:, :], tinyA[:, i:i + 1],
                                                    AF.Exp, scale=-0.5),
              waits=[(c_dv, rstd_dv_read[i]), (c_ac, a_ln)], inc=c_ac)
            a1s[i] = c_ac.at()
            st2_act_read[i] = a1s[i]
        return a1s

    def tile_norm_apply(i, src, dsts, a1, extra_waits=()):
        for dst in dsts:
            E("vector", lambda e, dst=dst: e.tensor_scalar(
                dst[:, :], src[:, :], st2[i][:, 0:1], rstd[i][:, 0:1], ALU.subtract, ALU.mult),
              waits=[(c_ac, a1)] + list(extra_waits), inc=c_dv)
        rstd_dv_read[i] = c_dv.at()
        return c_dv.at()

    def do_transposes(src_fn, src_waits, dst_fn, affine, dst_busy=()):
        for grp in range(4):
            bank = trA if grp % 2 == 0 else trB
            def tr(e, grp=grp, bank=bank):
                ins = None
                for cc in range(2):
                    c = grp * 2 + cc
                    for i in range(4):
                        ins = e.transpose(bank[:, (cc * 4 + i) * 128:(cc * 4 + i + 1) * 128],
                                          src_fn(i, c), ident[:, :])
                return ins
            E("tensor", tr, waits=list(src_waits) + [consts_ldx] + list(bfree(bank)),
              inc=c_pe)
            pe = c_pe.at()
            for cc in range(2):
                c = grp * 2 + cc
                dap = dst_fn(c)
                if affine is None:
                    E("scalar", lambda e, cc=cc, bank=bank, dap=dap: e.activation(
                        dap, bank[:, cc * 512:(cc + 1) * 512], AF.Copy),
                      waits=[(c_pe, pe), dst_busy], inc=c_ac)
                else:
                    g, bb = affine
                    E("vector", lambda e, c=c, cc=cc, bank=bank, dap=dap, g=g, bb=bb:
                      e.tensor_scalar(dap, bank[:, cc * 512:(cc + 1) * 512],
                                      g[:, c:c + 1], bb[:, c:c + 1], ALU.mult, ALU.add),
                      waits=[(c_pe, pe), dst_busy], inc=c_dv)
            if affine is None:
                bset(bank, (c_ac, c_ac.at()))
            else:
                bset(bank, (c_dv, c_dv.at()))
        return [(c_ac, c_ac.at()), (c_dv, c_dv.at())]

    def prologue_prime(L, p):
        tiles = list(range(p * PT, (p + 1) * PT))
        at_busy = buf_wait("at")
        hraw_busy = buf_wait("hraw")
        if L == 0:
            for i, t in enumerate(tiles):
                E("sync", lambda e, t=t, i=i: e.dma_start(
                    out=at[i][:, :], in_=x_h[t * 128:(t + 1) * 128, :]),
                  waits=[at_busy], inc=c_ldx)
            lx = c_ldx.at()
            a1s = tile_norm_stats([(i, at[i]) for i in range(PT)], extra_waits=[lx])
            for i, t in enumerate(tiles):
                tile_norm_apply(i, at[i], [at[i]], a1s[i])
                E("vector", lambda e, i=i: e.tensor_copy(hraw[i][:, :], at[i][:, :]),
                  waits=[hraw_busy, (c_dv, c_dv.at())], inc=c_dv)
                xn_done = c_dv.at()
                E("sync", lambda e, t=t, i=i: e.dma_start(
                    out=a_hbm[t * 128:(t + 1) * 128, :], in_=at[i][:, :]),
                  waits=[(c_dv, xn_done)], inc=c_st)
                buf_set(f"a_hbm_t{t}", *c_st.at())
                E("sync", lambda e, t=t, i=i: e.dma_start(
                    out=xn_hbm[t * 128:(t + 1) * 128, :], in_=hraw[i][:, :]),
                  waits=[(c_dv, xn_done)], inc=c_st)
                buf_set(f"xn_hbm_t{t}", *c_st.at())
            spill_w = c_st.at()
            buf_set("hraw", *(list(buf_wait("hraw")) + list(spill_w)))
            a1s = tile_norm_stats([(i, at[i]) for i in range(PT)])
            return tiles, a1s, [hraw_busy, spill_w]
        for i, t in enumerate(tiles):
            E("sync", lambda e, t=t, i=i: e.dma_start(
                out=at[i][:, :], in_=a_hbm[t * 128:(t + 1) * 128, :]),
              waits=[at_busy, buf_wait(f"a_hbm_t{t}")], inc=c_ldx)
        lx = c_ldx.at()
        a1s = tile_norm_stats([(i, at[i]) for i in range(PT)], extra_waits=[lx])
        return tiles, a1s, [hraw_busy]

    def prologue_finish(L, p, prime):
        tiles, a1s, extraw = prime
        h_done = 0
        for i in range(PT):
            h_done = tile_norm_apply(i, at[i], [hraw[i]], a1s[i], extra_waits=extraw)
        vr = do_transposes(lambda i, c: hraw[i][:, c * 128:(c + 1) * 128],
                           [(c_dv, h_done)], lambda c: h1T[:, c * 512:(c + 1) * 512],
                           (g1c, b1c), dst_busy=rtake("h1T"))
        rclear("h1T")
        buf_set("hraw", *vr)
        buf_set("at", (c_dv, h_done), *c_st.at())
        if dbg_h1T is not None and L == 0:
            E("sync", lambda e, p=p: e.dma_start(
                out=dbg_h1T[:, :, p * 512:(p + 1) * 512],
                in_=h1T[:, :].rearrange("q (c n) -> q c n", c=8)),
              waits=vr, inc=c_st)
            raddl("h1T", c_st.at())
        return vr

    def emit_prologue(L, p):
        return prologue_finish(L, p, prologue_prime(L, p))

    SL_WQ, SL_WK, SL_WV, SL_WO = 0, 8192, 16384, 24576
    SL_W1H, SL_W2H = 0, 16384
    SCL = float(DH) ** -0.5

    def load_wchunks(dst_col, src_h, rows0, rows1, cols0, cols1, waits=(), stages=None):
        ncols = cols1 - cols0
        st = {}
        for ci, r in enumerate(range(rows0, rows1, 128)):
            ap_in = src_h[r:r + 128, cols0:cols1]
            ap_out = arena[:, dst_col + ci * ncols: dst_col + (ci + 1) * ncols]
            E("gpsimd", lambda e, ap_out=ap_out, ap_in=ap_in: e.dma_start(
                out=ap_out, in_=ap_in), waits=waits, inc=c_ldw)
            waits = ()
            st[ci] = c_ldw.at()
        if stages is not None:
            return st
        return c_ldw.at()

    def mm_group(out_ap, pairs, waits, rank1=None, start_first=True, stop_last=True):
        pairs = tuple(pairs)
        waits = list(waits) + [consts_ldx, (c_gp, consts_gp)]

        def go(e, out_ap=out_ap, pairs=pairs, rank1=rank1):
            ins = None
            last = len(pairs) - 1
            for idx, (l, r) in enumerate(pairs):
                ins = e.matmul(out_ap, l, r, start=(idx == 0 and start_first),
                               stop=(stop_last and idx == last and rank1 is None))
            if rank1 is not None:
                l, r = rank1
                ins = e.matmul(out_ap, l, r, start=False, stop=stop_last)
            return ins
        E("tensor", go, waits=waits, inc=c_pe)
        return c_pe.at()

    rdmap = {}

    def radd(name, ch, v):
        lst = rdmap.setdefault(name, [])
        for i2, (c0, v0) in enumerate(lst):
            if c0 is ch:
                lst[i2] = (ch, max(v0, v))
                return
        lst.append((ch, v))

    def rtake(name):
        return list(rdmap.get(name, []))

    def rclear(name):
        rdmap[name] = []

    def raddl(name, pairs):
        for ch, v in pairs:
            radd(name, ch, v)

    def slot(t):
        return t % NVSLOT

    vready = {}
    h2T_spilled = {}
    mmrot = [0]
    mm_pool = [[mmA, mmB]]

    def set_mm_pool(banks):
        mm_pool[0] = list(banks)

    def next_mm():
        pool = mm_pool[0]
        bank = pool[mmrot[0] % len(pool)]
        mmrot[0] += 1
        return bank

    sm_guard = {}
    sme_guard = [[], []]
    katT_guard = [[], [], [], []]
    chat_guard = [[], []]
    qsm_guard = [[], []]
    h2b_guard = [[], []]

    def emit_qkv(L, p, h1T_ready, wts, interleave=None):
        def il(n=1):
            if interleave is not None:
                for _ in range(n):
                    interleave()
        wq_t, wk_t, wv_t, wo_t = wts
        res = {}
        if p == 0:
            res["kD_w"] = rtake("kDloc")
            rclear("kDloc")
            emit_qkv.kd_w = res["kD_w"]
        for i in range(PT):
            t = p * PT + i
            sl = slot(t)
            v_w = rtake(f"v{sl}")
            rclear(f"v{sl}")
            for dh in range(2):
                bank = next_mm()
                pairs = [(h1T[:, c * 512 + i * 128: c * 512 + (i + 1) * 128],
                          arena[:, SL_WV + c * 1024 + dh * 512: SL_WV + c * 1024 + (dh + 1) * 512])
                         for c in range(8)]
                pe = mm_group(bank[:, :], pairs,
                              waits=[h1T_ready, wv_t] + bfree(bank))
                dst = vtok[sl][:, dh * 512:(dh + 1) * 512]
                E("scalar", lambda e, bank=bank, dst=dst: e.activation(dst, bank[:, :], AF.Copy),
                  waits=[(c_pe, pe)] + v_w, inc=c_ac)
                bset(bank, (c_ac, c_ac.at()))
                il(2)
            vready[t] = c_ac.at()
        res["v"] = c_ac.at()
        qd_w = rtake("qDloc")
        rclear("qDloc")
        for m in range(4):
            bank = next_mm()
            pairs = [(arena[:, SL_WQ + c * 1024 + m * 128: SL_WQ + c * 1024 + (m + 1) * 128],
                      h1T[:, c * 512:(c + 1) * 512]) for c in range(8)]
            pe = mm_group(bank[:, :], pairs,
                          waits=[h1T_ready, wq_t] + bfree(bank))
            dst = qDloc[:, m * 512:(m + 1) * 512]
            E("scalar", lambda e, bank=bank, dst=dst: e.activation(
                dst, bank[:, :], AF.Copy, scale=SCL),
              waits=[(c_pe, pe)] + qd_w, inc=c_ac)
            bset(bank, (c_ac, c_ac.at()))
            il()
        res["qDloc"] = c_ac.at()
        for m in range(4):
            bank = next_mm()
            pairs = [(arena[:, SL_WK + c * 1024 + m * 128: SL_WK + c * 1024 + (m + 1) * 128],
                      h1T[:, c * 512:(c + 1) * 512]) for c in range(8)]
            pe = mm_group(bank[:, :], pairs,
                          waits=[h1T_ready, wk_t] + bfree(bank))
            dst = kDloc[:, m * T + p * 512: m * T + (p + 1) * 512]
            E("scalar", lambda e, bank=bank, dst=dst: e.activation(dst, bank[:, :], AF.Copy),
              waits=[(c_pe, pe)] + emit_qkv.kd_w, inc=c_ac)
            bset(bank, (c_ac, c_ac.at()))
            il()
        res["kDloc"] = c_ac.at()
        qdl_w = rtake("qDlin")
        rclear("qDlin")
        pend_trq = []

        def defer_trq(i, par, vq):
            def doit(i=i, par=par, vq=vq):
                bank_t = (trA, trB)[par]

                def trq(e, par=par, bank_t=bank_t):
                    ins = None
                    for h in range(4):
                        ins = e.transpose(bank_t[:, h * 128:(h + 1) * 128],
                                          qsm[par][:, h * 128:(h + 1) * 128], ident[:, :])
                    return ins
                E("tensor", trq, waits=[(c_ac, vq)] + bfree(bank_t), inc=c_pe)
                pe_t = c_pe.at()
                qsm_guard[par] = [(c_pe, pe_t)]
                dst = qDlin[:, i * 512:(i + 1) * 512]
                E("vector", lambda e, bank_t=bank_t, dst=dst: e.tensor_copy(
                    dst, bank_t[:, 0:512]), waits=[(c_pe, pe_t)] + qdl_w, inc=c_dv)
                bset(bank_t, (c_dv, c_dv.at()))
            pend_trq.append(doit)

        def pump_trq():
            if pend_trq:
                pend_trq.pop(0)()

        for i in range(PT):
            if i >= 2:
                pump_trq()
            bank = next_mm()
            pairs = [(h1T[:, c * 512 + i * 128: c * 512 + (i + 1) * 128],
                      arena[:, SL_WQ + c * 1024 + 512: SL_WQ + c * 1024 + 1024])
                     for c in range(8)]
            pe = mm_group(bank[:, :], pairs,
                          waits=[h1T_ready, wq_t] + bfree(bank))
            par = i % 2
            for h in range(4):
                col = i * 4 + h
                E("vector", lambda e, bank=bank, h=h, col=col: e.reduce_max(
                    sm_m[:, col:col + 1], bank[:, h * 128:(h + 1) * 128], axis=AX, negate=True),
                  waits=[(c_pe, pe)] + sm_guard.get(col, []), inc=c_dv)
            v_m = c_dv.at()
            for h in range(4):
                col = i * 4 + h
                E("scalar", lambda e, bank=bank, h=h, col=col, par=par: e.activation(
                    sm_e[par][:, h * 128:(h + 1) * 128], bank[:, h * 128:(h + 1) * 128],
                    AF.Exp, bias=sm_m[:, col:col + 1], accum_out=sm_s[:, col:col + 1]),
                  waits=[(c_dv, v_m)] + sme_guard[par], inc=c_ac)
            a_e = c_ac.at()
            bset(bank, (c_ac, a_e))
            E("vector", lambda e, i=i: e.reciprocal(
                sm_r[:, i * 4:(i + 1) * 4], sm_s[:, i * 4:(i + 1) * 4]),
              waits=[(c_ac, a_e)], inc=c_dv)
            v_r = c_dv.at()
            for h in range(4):
                col = i * 4 + h
                E("scalar", lambda e, h=h, col=col, par=par: e.activation(
                    qsm[par][:, h * 128:(h + 1) * 128], sm_e[par][:, h * 128:(h + 1) * 128],
                    AF.Copy, scale=sm_r[:, col:col + 1]),
                  waits=[(c_dv, v_r)] + qsm_guard[par], inc=c_ac)
            vq = c_ac.at()
            sme_guard[par] = [(c_ac, vq)]
            for h in range(4):
                sm_guard[i * 4 + h] = [(c_ac, vq)]
            defer_trq(i, par, vq)
            il()
        pump_trq()
        for i in range(PT):
            if i == 1:
                pump_trq()
            bank = next_mm()
            pairs = [(h1T[:, c * 512 + i * 128: c * 512 + (i + 1) * 128],
                      arena[:, SL_WK + c * 1024 + 512: SL_WK + c * 1024 + 1024])
                     for c in range(8)]
            pe = mm_group(bank[:, :], pairs,
                          waits=[h1T_ready, wk_t] + bfree(bank))
            par = i % 2
            E("vector", lambda e, bank=bank, par=par: e.tensor_scalar(
                qsm[par][:, :], bank[:, :], -1.0, 0.0, ALU.mult, ALU.max),
              waits=[(c_pe, pe)] + qsm_guard[par], inc=c_dv)
            v_relu = c_dv.at()
            E("scalar", lambda e, par=par: e.activation(
                sm_e[par][:, :], qsm[par][:, :], AF.Exp, scale=-1.0),
              waits=[(c_dv, v_relu)] + sme_guard[par], inc=c_ac)
            ae = c_ac.at()
            qsm_guard[par] = [(c_ac, ae)]
            kl_w = rtake(f"klin{i}")
            rclear(f"klin{i}")
            E("vector", lambda e, bank=bank, i=i, par=par: e.scalar_tensor_tensor(
                klin[i][:, :], bank[:, :], 0.0, sm_e[par][:, :], ALU.max, ALU.add),
              waits=[(c_ac, ae)] + kl_w, inc=c_dv)
            vk = c_dv.at()
            sme_guard[par] = [(c_dv, vk)]
            bset(bank, (c_dv, vk))
            il()
        res["klin"] = c_dv.at()
        res["qDlin"] = res["klin"]
        res["last_pe"] = c_pe.at()
        radd("h1T", c_pe, res["last_pe"])
        return res

    lin_state = {"rec_tgt": {}, "state_tgt": 0, "kcum_read": 0, "rec_read": {},
                 "state_zero": 0, "kcum_started": False}

    class AttnStepper:

        def __init__(self, L, p, qkv, ao_w, ut_w):
            self.L, self.p, self.qkv, self.ao_w, self.ut_w = L, p, qkv, ao_w, ut_w
            self.ci = 0
            self.bi = 0
            self.chain_t = {}
            self.dpool = [dA, dB, spare]

        def step_chain(self):
            if self.ci >= 16:
                return False
            n = self.ci
            self.ci += 1
            self._attn_item(n // 4, n % 4)
            return True

        def step_bucket(self):
            if self.bi >= 16:
                return False
            n = self.bi
            self.bi += 1
            if n % 2 == 0:
                self._lin_bucket_a(n // 2)
            else:
                self._lin_bucket_b(n // 2)
            return True

        def step(self):
            if self.step_chain():
                return True
            if self.step_bucket():
                self.step_bucket()
                return True
            return False

        def drain(self):
            while self.step():
                pass

        def _attn_item(self, i, h):
            L, p, qkv = self.L, self.p, self.qkv
            wb = p * PT + i
            sl4 = i * 4 + h
            dbank = self.dpool[sl4 % 3]
            col = 16 + sl4
            par = sl4 % 2
            mask_ap = masks[:, 0:256] if wb == 0 else masks[:, 256:512]
            q_ap = qDloc[:, h * 512 + i * 128: h * 512 + (i + 1) * 128]
            if wb == 0:
                k_ap = kDloc[:, h * T: h * T + 128]
            else:
                k_ap = kDloc[:, h * T + (wb - 1) * 128: h * T + (wb + 1) * 128]

            def dots(e, dbank=dbank, mask_ap=mask_ap, q_ap=q_ap, k_ap=k_ap, wb=wb):
                e.matmul(dbank[:, 0:256], ident[:, :], mask_ap, start=True, stop=False)
                if wb == 0:
                    return e.matmul(dbank[:, 128:256], q_ap, k_ap, start=False, stop=True)
                return e.matmul(dbank[:, 0:256], q_ap, k_ap, start=False, stop=True)
            E("tensor", dots,
              waits=[(c_ac, qkv["qDloc"]), (c_ac, qkv["kDloc"]),
                     consts_ldx, (c_gp, consts_gp)] + bfree(dbank), inc=c_pe)
            pe_d = c_pe.at()
            radd("qDloc", c_pe, pe_d)
            radd("kDloc", c_pe, pe_d)
            E("vector", lambda e, dbank=dbank, col=col: e.reduce_max(
                sm_m[:, col:col + 1], dbank[:, 0:256], axis=AX, negate=True),
              waits=[(c_pe, pe_d)] + sm_guard.get(col, []), inc=c_dv)
            v_m = c_dv.at()
            E("scalar", lambda e, dbank=dbank, col=col, par=par: e.activation(
                sm_e[par][:, 0:256], dbank[:, 0:256], AF.Exp,
                bias=sm_m[:, col:col + 1], accum_out=sm_s[:, col:col + 1]),
              waits=[(c_dv, v_m)] + sme_guard[par], inc=c_ac)
            a_e = c_ac.at()
            bset(dbank, (c_ac, a_e))
            E("vector", lambda e, col=col: e.reciprocal(
                sm_r[:, col:col + 1], sm_s[:, col:col + 1]),
              waits=[(c_ac, a_e)], inc=c_dv)
            v_r = c_dv.at()
            dst = ut[:, sl4 * 256:(sl4 + 1) * 256]
            E("scalar", lambda e, col=col, par=par, dst=dst: e.activation(
                dst, sm_e[par][:, 0:256], AF.Copy, scale=sm_r[:, col:col + 1]),
              waits=[(c_dv, v_r)] + self.ut_w, inc=c_ac)
            v_n = c_ac.at()
            sme_guard[par] = [(c_ac, v_n)]
            sm_guard[col] = [(c_ac, v_n)]
            self.chain_t[(i, h)] = v_n

        def _lin_bucket(self, j):
            self._lin_bucket_a(j)
            self._lin_bucket_b(j)

        def _lin_bucket_a(self, j):
            L, p, qkv = self.L, self.p, self.qkv
            ao_w = rtake("aoT")
            u = p * 8 + j
            i = j // 2
            half = j % 2
            t = p * PT + i
            for h in range(4):
                if u == 0:
                    dst = aoT[:, (4 + h) * 512: (4 + h) * 512 + 64]
                    E("gpsimd", lambda e, dst=dst: e.memset(dst, 0.0),
                      waits=ao_w, inc=c_gp)
                else:
                    par = (j * 4 + h) % 2
                    rcol = ((u - 1) % 2) * 4 + h
                    E("vector", lambda e, h=h, par=par, rcol=rcol: e.tensor_scalar_mul(
                        chat[par][:, :], ctx_state[:, h * 128:(h + 1) * 128],
                        recips[:, rcol:rcol + 1]),
                      waits=[(c_dv, lin_state["rec_tgt"].get(u - 1, 0)),
                             (c_gp, lin_state["state_zero"])] + chat_guard[par], inc=c_dv)
                    vc = c_dv.at()
                    lin_state["rec_read"][u - 1] = vc
                    obank = next_mm()
                    q_ap = qDlin[:, i * 512 + h * 128 + half * 64:
                                 i * 512 + h * 128 + half * 64 + 64]
                    E("tensor", lambda e, obank=obank, par=par, q_ap=q_ap: e.matmul(
                        obank[:, 0:64], chat[par][:, :], q_ap, start=True, stop=True),
                      waits=[(c_dv, vc), (c_dv, qkv["qDlin"])] + bfree(obank), inc=c_pe)
                    pe_o = c_pe.at()
                    chat_guard[par] = [(c_pe, pe_o)]
                    radd("qDlin", c_pe, pe_o)
                    dst = aoT[:, (4 + h) * 512 + j * 64: (4 + h) * 512 + (j + 1) * 64]
                    E("scalar", lambda e, obank=obank, dst=dst: e.activation(
                        dst, obank[:, 0:64], AF.Copy),
                      waits=[(c_pe, pe_o)] + ao_w, inc=c_ac)
                    bset(obank, (c_ac, c_ac.at()))

        def _lin_bucket_b(self, j):
            L, p, qkv = self.L, self.p, self.qkv
            u = p * 8 + j
            i = j // 2
            half = j % 2
            t = p * PT + i
            base = half * 64
            pe_ks = 0
            for h in range(4):
                cbank = (dA, dB)[h % 2]
                k_ap = klin[i][base:base + 64, h * 128:(h + 1) * 128]
                v_ap = vtok[slot(t)][base:base + 64, (4 + h) * 128:(5 + h) * 128]
                o_ap = ones_bf[base:base + 64, 0:1]

                def cmm(e, cbank=cbank, k_ap=k_ap, v_ap=v_ap, o_ap=o_ap, h=h):
                    e.matmul(cbank[:, 0:128], k_ap, v_ap, start=True, stop=True)
                    return e.matmul(kcum[:, h:h + 1], k_ap, o_ap, start=True,
                                    stop=True)
                E("tensor", cmm,
                  waits=[(c_dv, qkv["klin"]), (c_ac, vready[t]), (c_gp, consts_gp),
                         (c_dv, lin_state["kcum_read"])] + bfree(cbank) + bfree(kcum),
                  inc=c_pe)
                pe_c = c_pe.at()
                pe_ks = pe_c
                radd(f"klin{i}", c_pe, pe_c)
                radd(f"v{slot(t)}", c_pe, pe_c)
                E("vector", lambda e, cbank=cbank, h=h: e.scalar_tensor_tensor(
                    ctx_state[:, h * 128:(h + 1) * 128], cbank[:, 0:128], 1.0,
                    ctx_state[:, h * 128:(h + 1) * 128], ALU.mult, ALU.add),
                  waits=[(c_pe, pe_c), (c_gp, lin_state["state_zero"])], inc=c_dv)
                bset(cbank, (c_dv, c_dv.at()))
            lin_state["state_tgt"] = c_dv.at()
            rcol = (u % 2) * 4
            E("vector", lambda e: e.scalar_tensor_tensor(
                krun[:, :], kcum[:, 0:4], 1.0, krun[:, :], ALU.mult, ALU.add),
              waits=[(c_pe, pe_ks), (c_gp, lin_state["state_zero"])], inc=c_dv)
            lin_state["kcum_read"] = c_dv.at()
            E("vector", lambda e: e.tensor_scalar_add(krec_tmp[:, :], krun[:, :], 1e-6),
              waits=[(c_dv, lin_state["kcum_read"]),
                     (c_dv, lin_state["rec_read"].get(u - 1, 0))], inc=c_dv)
            E("vector", lambda e, rcol=rcol: e.reciprocal(
                recips[:, rcol:rcol + 4], krec_tmp[:, :]),
              waits=[(c_dv, c_dv.at())], inc=c_dv)
            lin_state["rec_tgt"][u] = c_dv.at()

    def emit_attnT_out(L, p, stepper, ao_w):
        items = [(i, h) for i in range(PT) for h in range(4)]

        def emit_tr(k):
            i, h = items[k]
            sl4 = i * 4 + h
            tbank = (trA, trB)[sl4 % 2]
            v_n = stepper.chain_t[(i, h)]
            base_c = sl4 * 256

            def trat(e, tbank=tbank, base_c=base_c):
                e.transpose(tbank[:, 0:128], ut[:, base_c:base_c + 128], ident[:, :])
                return e.transpose(tbank[:, 128:256],
                                   ut[:, base_c + 128:base_c + 256], ident[:, :])
            E("tensor", trat, waits=[(c_ac, v_n)] + bfree(tbank), inc=c_pe)
            pe_t = c_pe.at()
            radd("ut", c_pe, pe_t)
            return tbank, pe_t

        def emit_out(k, tbank, pe_t):
            i, h = items[k]
            sl4 = i * 4 + h
            par4 = sl4 % 4
            wb = p * PT + i
            E("vector", lambda e, tbank=tbank, par4=par4: e.tensor_copy(
                katT[par4][:, :], tbank[:, 0:256]),
              waits=[(c_pe, pe_t)] + katT_guard[par4], inc=c_dv)
            v_t = c_dv.at()
            bset(tbank, (c_dv, v_t))
            obank = next_mm()

            def outmm(e, obank=obank, par4=par4, wb=wb, h=h):
                ins = None
                first = True
                for jc in (0, 1):
                    kb = wb - 1 + jc
                    if kb < 0:
                        continue
                    ins = e.matmul(obank[:, 0:128],
                                   vtok[slot(kb)][:, h * 128:(h + 1) * 128],
                                   katT[par4][:, jc * 128:(jc + 1) * 128],
                                   start=first, stop=(jc == 1))
                    first = False
                return ins
            w_v = [(c_ac, vready[wb])]
            if wb > 0:
                w_v.append((c_ac, vready[wb - 1]))
            E("tensor", outmm, waits=[(c_dv, v_t)] + w_v + bfree(obank), inc=c_pe)
            pe_o = c_pe.at()
            katT_guard[par4] = [(c_pe, pe_o)]
            radd(f"v{slot(wb)}", c_pe, pe_o)
            if wb > 0:
                radd(f"v{slot(wb - 1)}", c_pe, pe_o)
            dst = aoT[:, h * 512 + i * 128: h * 512 + (i + 1) * 128]
            E("scalar", lambda e, obank=obank, dst=dst: e.activation(
                dst, obank[:, 0:128], AF.Copy),
              waits=[(c_pe, pe_o)] + ao_w, inc=c_ac)
            bset(obank, (c_ac, c_ac.at()))

        pend = emit_tr(0)
        for k in range(1, 16):
            nxt = emit_tr(k)
            emit_out(k - 1, *pend)
            pend = nxt
        emit_out(15, *pend)
        return c_ac.at()

    wres_free = [[], []]
    st2w_rd = [[] for _ in range(PT)]
    rstdw_rd = [[] for _ in range(PT)]

    def emit_wo(L, p, ao_done, wo_t, interleave=None):
        hraw_busy = list(buf_wait("hraw"))
        h_done = 0
        for i in range(PT):
            t = p * PT + i
            lw = {}
            for dh in (0, 1):
                wb_buf = wres[dh]
                E("sync", lambda e, t=t, dh=dh, wb_buf=wb_buf: e.dma_start(
                    out=wb_buf[:, :],
                    in_=a_hbm[t * 128:(t + 1) * 128, dh * 512:(dh + 1) * 512]),
                  waits=wres_free[dh] + list(buf_wait(f"a_hbm_t{t}")), inc=c_ldx)
                lw[dh] = c_ldx.at()
            st_t = {}
            for dh in (0, 1):
                wb_buf = wres[dh]
                bank = next_mm()
                pairs = [(aoT[:, c * 512 + i * 128: c * 512 + (i + 1) * 128],
                          arena[:, SL_WO + c * 1024 + dh * 512: SL_WO + c * 1024 + (dh + 1) * 512])
                         for c in range(8)]
                rank1 = (ones_bf[0:1, 0:128], borow[0:1, dh * 512:(dh + 1) * 512])
                pe = mm_group(bank[:, :], pairs,
                              waits=ao_done + [wo_t] + bfree(bank), rank1=rank1)
                radd("aoT", c_pe, pe)
                if interleave is not None:
                    interleave()
                    interleave()
                E("vector", lambda e, bank=bank, wb_buf=wb_buf: e.scalar_tensor_tensor(
                    wb_buf[:, :], bank[:, :], 1.0, wb_buf[:, :], ALU.mult, ALU.add),
                  waits=[(c_pe, pe), lw[dh]], inc=c_dv)
                bset(bank, (c_dv, c_dv.at()))
                E("sync", lambda e, t=t, dh=dh, wb_buf=wb_buf: e.dma_start(
                    out=a_hbm[t * 128:(t + 1) * 128, dh * 512:(dh + 1) * 512],
                    in_=wb_buf[:, :]),
                  waits=[(c_dv, c_dv.at())], inc=c_st)
                st_t[dh] = c_st.at()
            buf_set(f"a_hbm_t{t}", *c_st.at())
            E("vector", lambda e, i=i: e.bn_stats(st6w[i][:, 0:6], wres[0][:, :]))
            E("vector", lambda e, i=i: e.bn_stats(st6w[i][:, 6:12], wres[1][:, :]), inc=c_dv)
            v1 = c_dv.at()
            E("vector", lambda e, i=i: e.bn_aggr(st2w[i][:, :], st6w[i][:, :]),
              waits=[(c_dv, v1)] + st2w_rd[i], inc=c_dv)
            v2 = c_dv.at()
            E("scalar", lambda e, i=i: e.activation(tinyW[:, i:i + 1], st2w[i][:, 1:2], AF.Ln,
                                                    bias=eps_col[:, 0:1]),
              waits=[(c_dv, v2)], inc=c_ac)
            a_ln = c_ac.at()
            E("scalar", lambda e, i=i: e.activation(rstdw[i][:, :], tinyW[:, i:i + 1],
                                                    AF.Exp, scale=-0.5),
              waits=rstdw_rd[i] + [(c_ac, a_ln)], inc=c_ac)
            a1 = c_ac.at()
            for dh in (0, 1):
                dst = hraw[i][:, dh * 512:(dh + 1) * 512]
                E("vector", lambda e, i=i, dh=dh, dst=dst: e.tensor_scalar(
                    dst, wres[dh][:, :], st2w[i][:, 0:1], rstdw[i][:, 0:1],
                    ALU.subtract, ALU.mult),
                  waits=[(c_ac, a1)] + hraw_busy, inc=c_dv)
            h_done = c_dv.at()
            st2w_rd[i] = [(c_ac, a_ln), (c_dv, h_done)]
            rstdw_rd[i] = [(c_dv, h_done)]
            for dh in (0, 1):
                wres_free[dh] = [(c_dv, h_done)] + st_t[dh]
        buf_set("at", (c_dv, c_dv.at()), *c_st.at())
        vr = do_transposes(lambda i, c: hraw[i][:, c * 128:(c + 1) * 128],
                           [(c_dv, h_done)], lambda c: h1T[:, c * 512:(c + 1) * 512],
                           (g2c, b2c), dst_busy=rtake("h1T"))
        rclear("h1T")
        buf_set("hraw", *vr)
        E("sync", lambda e, p=p: e.dma_start(
            out=h2T_hbm[:, :, p * 512:(p + 1) * 512],
            in_=h1T[:, :].rearrange("q (c n) -> q c n", c=8)),
          waits=vr, inc=c_st)
        h2T_spilled[p] = c_st.at()
        raddl("h1T", h2T_spilled[p])

    SLQ = [(0, 8192), (16384, 24576)]

    def load_quarter(w1src, w2src, q, par, w1_waits, w2_waits):
        b1, b2 = SLQ[par]
        for ci in range(8):
            ap_in = w1src[ci * 128:(ci + 1) * 128, q * 1024:(q + 1) * 1024]
            ap_out = arena[:, b1 + ci * 1024: b1 + (ci + 1) * 1024]
            E("gpsimd", lambda e, ap_out=ap_out, ap_in=ap_in: e.dma_start(
                out=ap_out, in_=ap_in), waits=w1_waits, inc=c_ldw)
            w1_waits = ()
        t1 = c_ldw.at()
        for ci in range(8):
            ap_in = w2src[q * 1024 + ci * 128: q * 1024 + (ci + 1) * 128, 0:D]
            ap_out = arena[:, b2 + ci * 1024: b2 + (ci + 1) * 1024]
            E("gpsimd", lambda e, ap_out=ap_out, ap_in=ap_in: e.dma_start(
                out=ap_out, in_=ap_in), waits=w2_waits, inc=c_ldw)
            w2_waits = ()
        return t1, c_ldw.at()

    def emit_mlp(L, w1_guard, w2_guard):
        set_mm_pool([mmA, mmB, dA, dB, kcum])
        bset(kcum, (c_dv, lin_state["kcum_read"]),
             (c_dv, lin_state["rec_tgt"].get(31, 0)))
        wq_t = {}
        wq_t[0] = load_quarter(W1_h[L], W2_h[L], 0, 0, w1_guard, w1_guard)
        wq_t[1] = load_quarter(W1_h[L], W2_h[L], 1, 1, w2_guard, w2_guard)
        qlast = {}
        hb_tgt = {}
        for q in range(4):
            b1, b2 = SLQ[q % 2]
            w1_t, w2_t = wq_t[q]
            for sp in range(8):
                for spl in ([0, 1] if (sp == 0 and q == 0) else
                            ([(sp + 1) % 8] if not (q == 3 and sp == 7) else [])):
                    off2 = spl * 256
                    hb2 = h2b[spl % 2]
                    E("sync", lambda e, hb2=hb2, off2=off2: e.dma_start(
                        out=hb2[:, :].rearrange("q (c n) -> q c n", c=8),
                        in_=h2T_hbm[:, :, off2:off2 + 256]),
                      waits=h2T_spilled[spl // 2] + h2b_guard[spl % 2], inc=c_ldx)
                    hb_tgt[spl] = c_ldx.at()
                hb = h2b[sp % 2]
                lx = hb_tgt[sp]
                at_pre = {}
                for ii in (0, 1):
                    t = sp * 2 + ii
                    atb = at[2 * (sp % 2) + ii]
                    E("sync", lambda e, t=t, atb=atb: e.dma_start(
                        out=atb[:, :], in_=a_hbm[t * 128:(t + 1) * 128, :]),
                      waits=[buf_wait(f"a_hbm_t{t}"), buf_wait("at"),
                             buf_wait(f"atmlp{2 * (sp % 2) + ii}")],
                      inc=c_ldx)
                    at_pre[ii] = c_ldx.at()
                ut_w = rtake("ut")
                rclear("ut")
                for fc in range(8):
                    bank = next_mm()
                    pairs = [(arena[:, b1 + c * 1024 + fc * 128: b1 + c * 1024 + (fc + 1) * 128],
                              hb[:, c * 256:(c + 1) * 256]) for c in range(8)]
                    pe = mm_group(bank[:, 0:256], pairs,
                                  waits=[lx, w1_t] + bfree(bank))
                    h2b_guard[sp % 2] = [(c_pe, pe)]
                    bcol = b1cc[:, q * 8 + fc: q * 8 + fc + 1]
                    dst = ut[:, fc * 256:(fc + 1) * 256]
                    E("scalar", lambda e, bank=bank, dst=dst, bcol=bcol: e.activation(
                        dst, bank[:, 0:256], AF.Gelu, bias=bcol),
                      waits=[(c_pe, pe)] + ut_w, inc=c_ac)
                    bset(bank, (c_ac, c_ac.at()))
                ut_t = c_ac.at()
                last_w1_pe = c_pe.at()
                for ii in (0, 1):
                    t = sp * 2 + ii
                    atb = at[2 * (sp % 2) + ii]
                    lxa = at_pre[ii]
                    for dh in (0, 1):
                        bank = next_mm()
                        pairsA = [(ut[:, fc * 256 + ii * 128: fc * 256 + ii * 128 + 128],
                                   arena[:, b2 + fc * 1024 + dh * 512: b2 + fc * 1024 + (dh + 1) * 512])
                                  for fc in range(8)]
                        rank1 = (ones_bf[0:1, 0:128], b2row[0:1, dh * 512:(dh + 1) * 512]) \
                            if q == 0 else None
                        pe = mm_group(bank[:, :], pairsA,
                                      waits=[(c_ac, ut_t), w2_t] + bfree(bank),
                                      rank1=rank1)
                        radd("ut", c_pe, pe)
                        E("vector", lambda e, bank=bank, atb=atb, dh=dh: e.scalar_tensor_tensor(
                            atb[:, dh * 512:(dh + 1) * 512], bank[:, :], 1.0,
                            atb[:, dh * 512:(dh + 1) * 512], ALU.mult, ALU.add),
                          waits=[(c_pe, pe), lxa], inc=c_dv)
                        bset(bank, (c_dv, c_dv.at()))
                    vd = c_dv.at()
                    E("sync", lambda e, t=t, atb=atb: e.dma_start(
                        out=a_hbm[t * 128:(t + 1) * 128, :], in_=atb[:, :]),
                      waits=[(c_dv, vd)], inc=c_st)
                    buf_set(f"a_hbm_t{t}", *c_st.at())
                    buf_set(f"atmlp{2 * (sp % 2) + ii}", (c_dv, vd), *c_st.at())
            qlast[q] = (last_w1_pe, c_pe.at())
            if q == 0:
                guards["b2row_pe"] = c_pe.at()
            if q + 2 < 4:
                g = [(c_pe, qlast[q][1])]
                wq_t[q + 2] = load_quarter(W1_h[L], W2_h[L], q + 2, q % 2, g, g)
        buf_set("at", (c_dv, c_dv.at()), *c_st.at())
        guards["b1cc_ac"] = c_ac.at()
        return qlast[2][1], qlast[3][1]

    def emit_layer(L, wts, w1_guard, w2_guard):
        set_mm_pool([mmA, mmB])
        zero_w = [(c_dv, lin_state["rec_read"].get(30, 0)),
                  (c_dv, lin_state["state_tgt"]),
                  (c_dv, lin_state["kcum_read"]),
                  (c_dv, lin_state["rec_tgt"].get(31, 0))]
        E("gpsimd", lambda e: e.memset(ctx_state[:, :], 0.0), waits=zero_w, inc=c_gp)
        E("gpsimd", lambda e: e.memset(krun[:, :], 0.0), inc=c_gp)
        lin_state["state_zero"] = c_gp.at()
        vr = prologue_finish(L, 0, prologue_prime(L, 0))
        qkv_cur = emit_qkv(L, 0, vr, wts)
        prime_next = prologue_prime(L, 1)
        if dbg_q is not None and L == 0:
            E("sync", lambda e: e.dma_start(out=dbg_q, in_=qDloc[:, :]),
              waits=[(c_ac, qkv_cur["qDloc"])], inc=c_st)
            E("sync", lambda e: e.dma_start(out=dbg_ql, in_=qDlin[:, :]),
              waits=[(c_dv, qkv_cur["qDlin"])], inc=c_st)
            for i4 in range(4):
                E("sync", lambda e, i4=i4: e.dma_start(
                    out=dbg_kl[:, i4 * 512:(i4 + 1) * 512], in_=klin[i4][:, :]),
                  waits=[(c_dv, qkv_cur["klin"])], inc=c_st)
                E("sync", lambda e, i4=i4: e.dma_start(
                    out=dbg_v[:, i4 * D:(i4 + 1) * D], in_=vtok[i4][:, :]),
                  waits=[(c_ac, qkv_cur["v"])], inc=c_st)
                raddl(f"klin{i4}", c_st.at())
                raddl(f"v{i4}", c_st.at())
            E("sync", lambda e: e.dma_start(out=dbg_k, in_=kDloc[:, :]),
              waits=[(c_ac, qkv_cur["kDloc"])], inc=c_st)
            raddl("qDloc", c_st.at())
            raddl("kDloc", c_st.at())
            raddl("qDlin", c_st.at())
        ut_w0 = rtake("ut")
        rclear("ut")
        stepper_cur = AttnStepper(L, 0, qkv_cur, None, ut_w0)

        def make_pump(st):
            if st.ci < 16:
                return st.step
            last = [c_pe.at()]

            def pump():
                if c_pe.at() > last[0]:
                    if st.step_bucket():
                        last[0] = c_pe.at()
            return pump

        for p in range(NPANEL):
            qkv_next = None
            if p < NPANEL - 1:
                vr2 = prologue_finish(L, p + 1, prime_next)
                qkv_next = emit_qkv(L, p + 1, vr2, wts,
                                    interleave=make_pump(stepper_cur))
                if p < NPANEL - 2:
                    prime_next = prologue_prime(L, p + 2)
            stepper_cur.drain()
            ao_w = rtake("aoT")
            emit_attnT_out(L, p, stepper_cur, ao_w)
            ao_done = [(c_ac, c_ac.at()), (c_gp, c_gp.at())]
            stepper_next = None
            if p < NPANEL - 1:
                ut_wn = rtake("ut")
                rclear("ut")
                stepper_next = AttnStepper(L, p + 1, qkv_next, None, ut_wn)
            emit_wo(L, p, ao_done, wts[3],
                    interleave=(stepper_next.step_chain if stepper_next else None))
            if p == NPANEL - 1:
                guards["borow_pe"] = c_pe.at()
            if dbg_ao is not None and L == 0:
                E("sync", lambda e, p=p: e.dma_start(
                    out=dbg_ao[:, :, p * 512:(p + 1) * 512],
                    in_=aoT[:, :].rearrange("q (c n) -> q c n", c=8)),
                  waits=[(c_ac, c_ac.at()), (c_gp, c_gp.at())], inc=c_st)
                raddl("aoT", c_st.at())
            if qkv_next is not None:
                qkv_cur = qkv_next
                stepper_cur = stepper_next
        qkv_last = qkv_cur["last_pe"]
        return emit_mlp(L, [(c_pe, qkv_last)], [(c_pe, qkv_last), (c_pe, c_pe.at())])

    aff = load_layer_smalls(0)
    wts = (load_wchunks(SL_WQ, Wq_h[0], 0, D, 0, D),
           load_wchunks(SL_WK, Wk_h[0], 0, D, 0, D),
           load_wchunks(SL_WV, Wv_h[0], 0, D, 0, D),
           load_wchunks(SL_WO, Wo_h[0], 0, D, 0, D))

    def finish(last_tile_src=None):
        E("sync", lambda e: None, waits=c_st.at())
        b.finalize()
        return nc

    if stop_after == "prologue0":
        vr = emit_prologue(0, 0)
        for p in range(1, NPANEL):
            vr = emit_prologue(0, p)
        E("sync", lambda e: e.dma_start(out=out_h[0:128, :], in_=at[0][:, :]),
          waits=[(c_dv, c_dv.at())], inc=c_st)
        return finish()

    w1g, w2g = None, None
    nlayer = depth
    for L in range(nlayer):
        if L > 0:
            aff = load_layer_smalls(L)
            wts = (load_wchunks(SL_WQ, Wq_h[L], 0, D, 0, D, waits=[(c_pe, w1g)]),
                   load_wchunks(SL_WK, Wk_h[L], 0, D, 0, D, waits=[(c_pe, w1g)]),
                   load_wchunks(SL_WV, Wv_h[L], 0, D, 0, D, waits=[(c_pe, w2g)]),
                   load_wchunks(SL_WO, Wo_h[L], 0, D, 0, D, waits=[(c_pe, w2g)]))
        _r = emit_layer(L, wts, None, None)
        if _r == "STOP" or (isinstance(_r, tuple) and len(_r) == 2 and _r[0] == "STOP"):
            return finish()
        w1g, w2g = _r
        if stop_after == f"layer{L}":
            E("sync", lambda e: e.dma_start(out=out_h[0:128, :], in_=at[0][:, :]),
              waits=[(c_dv, c_dv.at())], inc=c_st)
            return finish()

    if stop_after == "attnmod":
        E("sync", lambda e: e.dma_start(out=out_h[0:128, :], in_=at[0][:, :]),
          waits=[(c_dv, c_dv.at())], inc=c_st)
        return finish()

    def xnt_chunk(c):
        if c < 4:
            return kDloc[:, c * 2048:(c + 1) * 2048]
        if c < 6:
            return h1T[:, (c - 4) * 2048:(c - 3) * 2048]
        return aoT[:, (c - 6) * 2048:(c - 5) * 2048]

    xnt_busy = (rtake("kDloc") + rtake("h1T") + rtake("aoT") +
                 [(c_pe, c_pe.at())] + c_st.at())
    xnt_ready = []
    for g4 in range(4):
        for i in range(4):
            t = g4 * 4 + i
            E("sync", lambda e, t=t, i=i: e.dma_start(
                out=hraw[i][:, :], in_=xn_hbm[t * 128:(t + 1) * 128, :]),
              waits=list(buf_wait("hraw")) + list(buf_wait(f"xn_hbm_t{t}")), inc=c_ldx)
        lxg = c_ldx.at()
        vr = do_transposes(lambda i, c: hraw[i][:, c * 128:(c + 1) * 128],
                           [lxg],
                           lambda c, g4=g4: xnt_chunk(c)[:, g4 * 512:(g4 + 1) * 512],
                           None, dst_busy=xnt_busy)
        xnt_busy = ()
        buf_set("hraw", *vr)
        xnt_ready = vr
    xnt_ld_w = xnt_ready

    E("gpsimd", lambda e: e.dma_start(
        out=gw_sb[:, :].rearrange("p (c n) -> p c n", n=4),
        in_=gate_w_h.rearrange("(c p) n -> p c n", p=128)), inc=c_ldw)
    E("gpsimd", lambda e: e.dma_start(out=gb_row[:, :], in_=gate_b_h), inc=c_ldw)
    gw_ld = c_ldw.at()

    for tt in range(NT):
        pairs = [(xnt_chunk(c)[:, tt * 128:(tt + 1) * 128], gw_sb[:, c * 4:(c + 1) * 4])
                 for c in range(8)]
        rank1 = (ones_bf[0:1, 0:128], gb_row[0:1, 0:4])
        pe = mm_group(spare[:, 0:4], pairs,
                      waits=xnt_ld_w + [gw_ld] + bfree(spare), rank1=rank1)
        col = tt % 16
        E("vector", lambda e, col=col: e.reduce_max(
            sm_m[:, col:col + 1], spare[:, 0:4], axis=AX, negate=True),
          waits=[(c_pe, pe)] + sm_guard.get(col, []), inc=c_dv)
        v_m = c_dv.at()
        E("scalar", lambda e, col=col: e.activation(
            gsc[:, 0:4], spare[:, 0:4], AF.Exp,
            bias=sm_m[:, col:col + 1], accum_out=sm_s[:, col:col + 1]),
          waits=[(c_dv, v_m), (c_dv, c_dv.at())], inc=c_ac)
        a_e = c_ac.at()
        bset(spare, (c_ac, a_e))
        sm_guard[col] = [(c_ac, a_e)]
        E("vector", lambda e, col=col: e.reciprocal(sm_r[:, col:col + 1], sm_s[:, col:col + 1]),
          waits=[(c_ac, a_e)], inc=c_dv)
        v_r = c_dv.at()
        E("vector", lambda e, tt=tt, col=col: e.tensor_scalar_mul(
            gwt[:, tt * 4:(tt + 1) * 4], gsc[:, 0:4], sm_r[:, col:col + 1]),
          waits=[(c_dv, v_r)], inc=c_dv)
        g0 = gwt[:, tt * 4 + 0:tt * 4 + 1]
        g1 = gwt[:, tt * 4 + 1:tt * 4 + 2]
        g2 = gwt[:, tt * 4 + 2:tt * 4 + 3]
        g3 = gwt[:, tt * 4 + 3:tt * 4 + 4]
        def step(fn):
            E("vector", fn, waits=[(c_dv, c_dv.at())], inc=c_dv)
        step(lambda e, g0=g0, g1=g1: e.tensor_tensor(gsc[:, 0:1], g0, g1, ALU.max))
        step(lambda e, g2=g2, g3=g3: e.tensor_tensor(gsc[:, 1:2], g2, g3, ALU.max))
        step(lambda e, g0=g0, g1=g1: e.tensor_tensor(gsc[:, 2:3], g0, g1, ALU.min))
        step(lambda e, g2=g2, g3=g3: e.tensor_tensor(gsc[:, 3:4], g2, g3, ALU.min))
        step(lambda e: e.tensor_tensor(gsc[:, 4:5], gsc[:, 0:1], gsc[:, 1:2], ALU.min))
        step(lambda e: e.tensor_tensor(gsc[:, 5:6], gsc[:, 2:3], gsc[:, 3:4], ALU.max))
        step(lambda e: e.tensor_tensor(gsc[:, 6:7], gsc[:, 4:5], gsc[:, 5:6], ALU.max))
        step(lambda e, tt=tt: e.tensor_scalar(
            gsc[:, 0:4], gwt[:, tt * 4:(tt + 1) * 4], gsc[:, 6:7], None, ALU.is_ge))
        step(lambda e, tt=tt: e.tensor_tensor(
            scg[:, tt * 4:(tt + 1) * 4], gsc[:, 0:4], gwt[:, tt * 4:(tt + 1) * 4],
            ALU.mult))
    gate_done = c_dv.at()
    if "gate" in taps:
        dbg_gate = dram("dbg_gate", [128, 128], F32, kind="ExternalOutput")
        E("sync", lambda e: e.dma_start(out=dbg_gate[:, 0:64], in_=gwt[:, :]),
          waits=[(c_dv, gate_done)], inc=c_st)
        E("sync", lambda e: e.dma_start(out=dbg_gate[:, 64:128], in_=scg[:, :]),
          waits=[(c_dv, gate_done)], inc=c_st)

    if stop_after == "gate":
        E("sync", lambda e: e.dma_start(out=out_h[0:128, :], in_=at[0][:, :]),
          waits=[(c_dv, c_dv.at())], inc=c_st)
        return finish()

    set_mm_pool([mmA, mmB, dA, dB, kcum])
    experts = [("s", 0), ("s", 1), ("r", 0), ("r", 1), ("r", 2), ("r", 3)]
    acc_guard = {}
    b1cc_par_read = [c_ac.at(), 0]
    qstream = [(ei, q) for ei in range(6) for q in range(4)]

    def exp_srcs(ei):
        kind, ex = experts[ei]
        return (sw1_h[ex] if kind == "s" else rw1_h[ex],
                sw2_h[ex] if kind == "s" else rw2_h[ex])

    eq_t = {}

    def load_eq(gq, waits1, waits2):
        ei, q = qstream[gq]
        w1src, w2src = exp_srcs(ei)
        eq_t[gq] = load_quarter(w1src, w2src, q, gq % 2, waits1, waits2)
    load_eq(0, [(c_pe, w1g)], [(c_pe, w1g)])
    load_eq(1, [(c_pe, w2g)], [(c_pe, w2g)])
    bias_ok_map = {}

    def emit_expert_bias(ei):
        kind, ex = experts[ei]
        boff = (ei % 2) * 32
        b1src = sb1_h[ex:ex + 1, :] if kind == "s" else rb1_h[ex:ex + 1, :]
        b2src = sb2_h[ex:ex + 1, :] if kind == "s" else rb2_h[ex:ex + 1, :]
        E("gpsimd", lambda e, b1src=b1src: e.dma_start(out=b1row[:, :], in_=b1src),
          waits=[(c_pe, guards["rows_pe"])], inc=c_ldw)
        E("gpsimd", lambda e, b2src=b2src: e.dma_start(out=b2row[:, :], in_=b2src),
          waits=[(c_pe, guards["b2row_pe"])], inc=c_ldw)
        wb_t = c_ldw.at()

        def tr_b1(e):
            ins = None
            for c in range(32):
                ins = e.matmul(spare[:, 32 + c:33 + c], b1row[0:1, c * 128:(c + 1) * 128],
                               ones_bf[0:1, 0:1], start=(c == 0), stop=(c == 31))
            return ins
        E("tensor", tr_b1, waits=[wb_t, (c_ac, b1cc_par_read[ei % 2])] + bfree(spare),
          inc=c_pe)
        pe_b = c_pe.at()
        guards["rows_pe"] = pe_b
        E("vector", lambda e, boff=boff: e.tensor_copy(
            b1cc[:, boff:boff + 32], spare[:, 32:64]),
          waits=[(c_pe, pe_b)], inc=c_dv)
        bset(spare, (c_dv, c_dv.at()))
        bias_ok_map[ei] = c_dv.at()
    emit_expert_bias(0)
    for gq, (ei, q) in enumerate(qstream):
        kind, ex = experts[ei]
        boff = (ei % 2) * 32
        bias_ok = bias_ok_map[ei]
        b1, b2 = SLQ[gq % 2]
        w1_t, w2_t = eq_t[gq]
        for sp in range(8):
            init = (gq == 0)
            at_pre = {}
            for ii in (0, 1):
                t = sp * 2 + ii
                atb = at[2 * (sp % 2) + ii]
                if not init:
                    E("sync", lambda e, t=t, atb=atb: e.dma_start(
                        out=atb[:, :], in_=acc_hbm[t * 128:(t + 1) * 128, :]),
                      waits=[acc_guard.get(t, None), buf_wait("at"),
                             buf_wait(f"atacc{2 * (sp % 2) + ii}")],
                      inc=c_ldx)
                at_pre[ii] = c_ldx.at()
            ut_w = rtake("ut")
            rclear("ut")
            for fc in range(8):
                bank = next_mm()
                pairs = [(arena[:, b1 + c * 1024 + fc * 128: b1 + c * 1024 + (fc + 1) * 128],
                          xnt_chunk(c)[:, sp * 256:(sp + 1) * 256]) for c in range(8)]
                pe = mm_group(bank[:, 0:256], pairs,
                              waits=xnt_ld_w + [w1_t] + bfree(bank))
                bcol = b1cc[:, boff + q * 8 + fc: boff + q * 8 + fc + 1]
                dst = ut[:, fc * 256:(fc + 1) * 256]
                E("scalar", lambda e, bank=bank, dst=dst, bcol=bcol: e.activation(
                    dst, bank[:, 0:256], AF.Relu, bias=bcol),
                  waits=[(c_pe, pe), (c_dv, bias_ok)] + ut_w, inc=c_ac)
                bset(bank, (c_ac, c_ac.at()))
            ut_t = c_ac.at()
            b1cc_par_read[ei % 2] = ut_t
            for ii in (0, 1):
                t = sp * 2 + ii
                atb = at[2 * (sp % 2) + ii]
                lxa = at_pre[ii]
                for dh in (0, 1):
                    bank = next_mm()
                    pairsA = [(ut[:, fc * 256 + ii * 128: fc * 256 + ii * 128 + 128],
                               arena[:, b2 + fc * 1024 + dh * 512: b2 + fc * 1024 + (dh + 1) * 512])
                              for fc in range(8)]
                    rank1 = (ones_bf[0:1, 0:128], b2row[0:1, dh * 512:(dh + 1) * 512]) \
                        if q == 0 else None
                    pe = mm_group(bank[:, :], pairsA,
                                  waits=[(c_ac, ut_t), w2_t] + bfree(bank),
                                  rank1=rank1)
                    radd("ut", c_pe, pe)
                    if kind == "s":
                        if init:
                            E("vector", lambda e, bank=bank, atb=atb, dh=dh: e.tensor_scalar_mul(
                                atb[:, dh * 512:(dh + 1) * 512], bank[:, :], 0.5),
                              waits=[(c_pe, pe), buf_wait("at"),
                                     buf_wait(f"atacc{2 * (sp % 2) + ii}")],
                              inc=c_dv)
                        else:
                            E("vector", lambda e, bank=bank, atb=atb, dh=dh: e.scalar_tensor_tensor(
                                atb[:, dh * 512:(dh + 1) * 512], bank[:, :], 0.5,
                                atb[:, dh * 512:(dh + 1) * 512], ALU.mult, ALU.add),
                              waits=[(c_pe, pe), lxa], inc=c_dv)
                    else:
                        sc_col = scg[:, t * 4 + ex: t * 4 + ex + 1]
                        E("vector", lambda e, bank=bank, atb=atb, dh=dh, sc_col=sc_col:
                          e.scalar_tensor_tensor(
                              atb[:, dh * 512:(dh + 1) * 512], bank[:, :], sc_col,
                              atb[:, dh * 512:(dh + 1) * 512], ALU.mult, ALU.add),
                          waits=[(c_pe, pe), lxa, (c_dv, gate_done)], inc=c_dv)
                    bset(bank, (c_dv, c_dv.at()))
                vd = c_dv.at()
                E("sync", lambda e, t=t, atb=atb: e.dma_start(
                    out=acc_hbm[t * 128:(t + 1) * 128, :], in_=atb[:, :]),
                  waits=[(c_dv, vd)], inc=c_st)
                acc_guard[t] = c_st.at()
                buf_set(f"atacc{2 * (sp % 2) + ii}", (c_dv, vd), *c_st.at())
        if q == 0:
            guards["b2row_pe"] = c_pe.at()
            if ei + 1 < len(experts):
                emit_expert_bias(ei + 1)
        if gq + 2 < len(qstream):
            g = [(c_pe, c_pe.at())]
            load_eq(gq + 2, g, g)

    if stop_after == "ffn":
        E("sync", lambda e: e.dma_start(out=out_h[0:128, :], in_=at[0][:, :]),
          waits=[(c_dv, c_dv.at())], inc=c_st)
        return finish()

    comb_prev = list(buf_wait("at"))
    for k in range(4):
        comb_prev += list(buf_wait(f"atacc{k}"))
    for tt in range(NT):
        E("sync", lambda e, tt=tt: e.dma_start(
            out=at[0][:, :], in_=acc_hbm[tt * 128:(tt + 1) * 128, :]),
          waits=[acc_guard.get(tt, None)] + comb_prev, inc=c_ldx)
        E("sync", lambda e, tt=tt: e.dma_start(
            out=at[2][:, :], in_=x_h[tt * 128:(tt + 1) * 128, :]), inc=c_ldx)
        E("sync", lambda e, tt=tt: e.dma_start(
            out=at[3][:, :], in_=a_hbm[tt * 128:(tt + 1) * 128, :]),
          waits=[buf_wait(f"a_hbm_t{tt}")], inc=c_ldx)
        lx = c_ldx.at()
        E("scalar", lambda e: e.activation(at[1][:, :], at[0][:, :], AF.Silu),
          waits=[lx], inc=c_ac)
        a_s = c_ac.at()
        E("vector", lambda e: e.tensor_mul(at[0][:, :], at[1][:, :], at[3][:, :]),
          waits=[(c_ac, a_s)], inc=c_dv)
        v_p = c_dv.at()
        E("vector", lambda e: e.tensor_add(at[1][:, :], at[0][:, :], at[2][:, :]),
          waits=[(c_dv, v_p)], inc=c_dv)
        v_c = c_dv.at()
        E("sync", lambda e, tt=tt: e.dma_start(
            out=out_h[tt * 128:(tt + 1) * 128, :], in_=at[1][:, :]),
          waits=[(c_dv, v_c)], inc=c_st)
        comb_prev = [(c_dv, v_c)] + c_st.at()

    return finish()


def make_consts():
    ident = np.eye(128, dtype=np.float32).astype(ml_dtypes.bfloat16)
    m = np.zeros((128, 512), np.float32)
    causal = np.triu(np.full((128, 128), NEG, np.float32), 1)
    m[:, 0:128] = NEG
    m[:, 128:256] = causal
    m[:, 256:384] = 0.0
    m[:, 384:512] = causal
    return ident, m.astype(ml_dtypes.bfloat16)


def make_in_maps(inputs):
    ident, masks = make_consts()
    x = np.ascontiguousarray(np.asarray(inputs["L0_attn_x"], dtype=np.float32))
    f32c = lambda v: np.ascontiguousarray(np.asarray(v, dtype=np.float32))
    common = {
        "Wq": f32c(inputs["L0_attn_Wq"]), "Wk": f32c(inputs["L0_attn_Wk"]),
        "Wv": f32c(inputs["L0_attn_Wv"]), "Wo": f32c(inputs["L0_attn_Wo"]),
        "bo": f32c(inputs["L0_attn_bo"]),
        "ln1_g": f32c(inputs["L0_attn_ln1_g"]), "ln1_b": f32c(inputs["L0_attn_ln1_b"]),
        "W1": f32c(inputs["L0_attn_W1"]), "b1": f32c(inputs["L0_attn_b1"]),
        "W2": f32c(inputs["L0_attn_W2"]), "b2": f32c(inputs["L0_attn_b2"]),
        "ln2_g": f32c(inputs["L0_attn_ln2_g"]), "ln2_b": f32c(inputs["L0_attn_ln2_b"]),
        "gate_w": f32c(inputs["L0_ffn_gate_w"]),
        "gate_b": f32c(inputs["L0_ffn_gate_b"]).reshape(1, 4),
        "sw1": f32c(inputs["L0_ffn_sw1"]), "sb1": f32c(inputs["L0_ffn_sb1"]),
        "sw2": f32c(inputs["L0_ffn_sw2"]), "sb2": f32c(inputs["L0_ffn_sb2"]),
        "rw1": f32c(inputs["L0_ffn_rw1"]), "rb1": f32c(inputs["L0_ffn_rb1"]),
        "rw2": f32c(inputs["L0_ffn_rw2"]), "rb2": f32c(inputs["L0_ffn_rb2"]),
        "ident": ident, "masks": masks,
    }
    in_maps = []
    for core in range(8):
        m = dict(common)
        m["x"] = np.ascontiguousarray(x[core])
        in_maps.append(m)
    return in_maps


def kernel(**inputs):
    nc = build_program()
    in_maps = make_in_maps(inputs)
    res = run_bass_kernel_spmd(nc, in_maps, core_ids=list(range(8)))
    return np.stack([r["out"] for r in res.results], axis=0)


# version 17
# speedup vs baseline: 1.0328x; 1.0328x over previous
import numpy as np
import ml_dtypes
import concourse.bass as bass
import concourse.mybir as mybir
from concourse.bass_utils import run_bass_kernel_spmd

F32 = mybir.dt.float32
BF16 = mybir.dt.bfloat16
AF = mybir.ActivationFunctionType
ALU = mybir.AluOpType
AX = mybir.AxisListType.X

T = 2048
D = 1024
NT = 16
NPANEL = 4
PT = 4
DEPTH = 4
HEADS = 8
DH = 128
NLOC = 4
WINDOW = 128
BUCKET = 64
FF = 4096
EPS = 1e-5
NEG = -1e30


class Chan:
    def __init__(self, sem, step=1):
        self.sem = sem
        self.step = step
        self.count = 0

    def at(self):
        return self.count


class DmaChan:

    R = 8

    def __init__(self, subs):
        self.subs = subs
        self.k = 0

    def issue_waits(self):
        sl = self.k % self.R
        base = 16 * (self.k // self.R)
        return [(self.subs[sl], base)] if base > 0 else []

    def advance(self):
        sl = self.k % self.R
        self.subs[sl].count += 16
        self.k += 1
        return self.subs[sl]

    def at(self):
        return [(sub, sub.count) for sub in self.subs if sub.count > 0]


class Builder:
    ENGS = ("tensor", "vector", "scalar", "gpsimd", "sync")

    def __init__(self, nc):
        self.nc = nc
        self.q = {e: [] for e in self.ENGS}
        self.meta = {e: [] for e in self.ENGS}
        self.waited = {e: {} for e in self.ENGS}

    def emit(self, eng, fn, waits=(), inc=None, inc_n=None, tag=None):
        if isinstance(inc, DmaChan):
            waits = list(waits) + inc.issue_waits()
            sub = inc.subs[inc.k % inc.R]
            inc.k += 1
            inc = sub
            inc_n = 16
        w = []
        seen = self.waited[eng]
        flat = []
        for item in waits:
            if item is None:
                continue
            if isinstance(item, tuple) and len(item) == 2 and isinstance(item[0], Chan):
                flat.append(item)
            else:
                for sub in item:
                    if sub is not None:
                        flat.append(sub)
        w_chans = []
        for ch, v in flat:
            if v <= 0 or seen.get(ch, 0) >= v:
                continue
            seen[ch] = v
            w.append((ch.sem, v))
            w_chans.append((ch, v))
        n = None
        if inc is not None:
            n = inc_n if inc_n is not None else inc.step
            inc.count += n
        w = tuple(w)

        def run(e, fn=fn, w=w, inc=inc, n=n):
            for sem, v in w:
                e.wait_ge(sem, v)
            ins = fn(e)
            if inc is not None:
                ins.then_inc(inc.sem, n)
        self.q[eng].append(run)
        self.meta[eng].append((tuple(w_chans), (inc, n)))
        if tag is not None:
            self.tags = getattr(self, "tags", [])
            self.tags.append((tag, eng, tuple((id(ch), v) for ch, v in w_chans),
                              (id(inc) if inc else None, n)))
        return inc.count if inc is not None else None

    def check(self):
        pos = {e: 0 for e in self.ENGS}
        counts = {}
        progress = True
        total = sum(len(self.meta[e]) for e in self.ENGS)
        done = 0
        while progress:
            progress = False
            for e in self.ENGS:
                while pos[e] < len(self.meta[e]):
                    waits, (inc, n) = self.meta[e][pos[e]]
                    if any(counts.get(id(ch), 0) < v for ch, v in waits):
                        break
                    if inc is not None:
                        counts[id(inc)] = counts.get(id(inc), 0) + n
                    pos[e] += 1
                    done += 1
                    progress = True
        if done != total:
            detail = {e: (pos[e], len(self.meta[e])) for e in self.ENGS}
            stuck = {}
            for e in self.ENGS:
                if pos[e] < len(self.meta[e]):
                    waits, _ = self.meta[e][pos[e]]
                    stuck[e] = [(getattr(ch.sem, "name", "?"), v, counts.get(id(ch), 0))
                                for ch, v in waits]
            raise RuntimeError(f"DEADLOCK: progress {detail}; stuck on {stuck}")
        return counts

    def finalize(self):
        self.check()
        with self.nc.Block() as blk:
            for eng in self.ENGS:
                if not self.q[eng]:
                    continue

                def runner(e, eng=eng):
                    for f in self.q[eng]:
                        f(e)
                getattr(blk, eng)(runner)


def build_program(taps=(), depth=DEPTH, stop_after=None):
    nc = bass.Bass("TRN2", target_bir_lowering=False)
    b = Builder(nc)
    from contextlib import ExitStack
    es = ExitStack()
    nc._keepalive_es = es

    def dram(name, shape, dtype, kind="ExternalInput"):
        return nc.dram_tensor(name, shape, dtype, kind=kind).ap()

    def internal(name, shape, dtype, tapname=None):
        kind = "ExternalOutput" if (tapname in taps) else "Internal"
        return nc.dram_tensor(name, shape, dtype, kind=kind).ap()

    sbuf = lambda name, shape, dtype: nc.alloc_sbuf_tensor(name, shape, dtype)
    psum = lambda name, shape, dtype=F32: nc.alloc_psum_tensor(name, shape, dtype)

    def chan(name, step=1):
        return Chan(es.enter_context(nc.semaphore(name)), step)

    x_h = dram("x", [T, D], F32)
    Wq_h = dram("Wq", [DEPTH, D, D], F32)
    Wk_h = dram("Wk", [DEPTH, D, D], F32)
    Wv_h = dram("Wv", [DEPTH, D, D], F32)
    Wo_h = dram("Wo", [DEPTH, D, D], F32)
    bo_h = dram("bo", [DEPTH, D], F32)
    g1_h = dram("ln1_g", [DEPTH, D], F32)
    b1_h = dram("ln1_b", [DEPTH, D], F32)
    W1_h = dram("W1", [DEPTH, D, FF], F32)
    bb1_h = dram("b1", [DEPTH, FF], F32)
    W2_h = dram("W2", [DEPTH, FF, D], F32)
    bb2_h = dram("b2", [DEPTH, D], F32)
    g2_h = dram("ln2_g", [DEPTH, D], F32)
    b2ln_h = dram("ln2_b", [DEPTH, D], F32)
    gate_w_h = dram("gate_w", [D, 4], F32)
    gate_b_h = dram("gate_b", [1, 4], F32)
    sw1_h = dram("sw1", [2, D, FF], F32)
    sb1_h = dram("sb1", [2, FF], F32)
    sw2_h = dram("sw2", [2, FF, D], F32)
    sb2_h = dram("sb2", [2, D], F32)
    rw1_h = dram("rw1", [4, D, FF], F32)
    rb1_h = dram("rb1", [4, FF], F32)
    rw2_h = dram("rw2", [4, FF, D], F32)
    rb2_h = dram("rb2", [4, D], F32)
    ident_h = dram("ident", [128, 128], BF16)
    masks_h = dram("masks", [128, 512], BF16)
    out_h = dram("out", [T, D], F32, kind="ExternalOutput")

    a_hbm = internal("a_scratch", [T, D], F32, "a")
    xn_hbm = internal("xn_scratch", [T, D], BF16, "xnT")
    h2T_hbm = internal("h2T_scratch", [128, 8, T], BF16, "h2T")
    acc_hbm = internal("acc_scratch", [T, D], F32, "acc")
    dbg_h1T = internal("dbg_h1T", [128, 8, T], BF16, "h1T") if "h1T" in taps else None
    if "qkv" in taps:
        dbg_q = dram("dbg_q", [128, 2048], BF16, kind="ExternalOutput")
        dbg_k = dram("dbg_k", [128, 4 * T], BF16, kind="ExternalOutput")
        dbg_ql = dram("dbg_ql", [128, 2048], BF16, kind="ExternalOutput")
        dbg_kl = dram("dbg_kl", [128, 4 * 512], BF16, kind="ExternalOutput")
        dbg_v = dram("dbg_v", [128, 4 * D], BF16, kind="ExternalOutput")
    else:
        dbg_q = dbg_k = dbg_ql = dbg_kl = dbg_v = None
    dbg_ao = internal("dbg_ao", [128, 8, T], BF16, "ao") if "ao" in taps else None
    dbg_att = dram("dbg_att", [128, 1024], F32, kind="ExternalOutput") if "att" in taps else None
    dbg_lin = dram("dbg_lin", [128, 1024], F32, kind="ExternalOutput") if "lin" in taps else None

    at = [sbuf(f"at{i}", [128, D], F32) for i in range(PT)]
    st6 = [sbuf(f"st6_{i}", [128, 12], F32) for i in range(PT)]
    st2 = [sbuf(f"st2_{i}", [128, 2], F32) for i in range(PT)]
    rstd = [sbuf(f"rstd{i}", [128, 1], F32) for i in range(PT)]
    tinyA = sbuf("tinyA", [128, 4], F32)
    st6w = [sbuf(f"st6w_{i}", [128, 12], F32) for i in range(PT)]
    st2w = [sbuf(f"st2w_{i}", [128, 2], F32) for i in range(PT)]
    rstdw = [sbuf(f"rstdw{i}", [128, 1], F32) for i in range(PT)]
    tinyW = sbuf("tinyW", [128, 4], F32)
    hraw = [sbuf(f"hraw{i}", [128, D], BF16) for i in range(PT)]
    h1T = sbuf("h1T", [128, 8 * 512], BF16)
    aoT = sbuf("aoT", [128, 8 * 512], BF16)
    qDloc = sbuf("qDloc", [128, 4 * 512], BF16)
    kDloc = sbuf("kDloc", [128, 4 * T], BF16)
    qDlin = sbuf("qDlin", [128, 4 * 512], BF16)
    klin = [sbuf(f"klin{i}", [128, 512], BF16) for i in range(PT)]
    NVSLOT = 9
    vtok = [sbuf(f"vtok{i}", [128, D], BF16) for i in range(NVSLOT)]
    ctx_state = sbuf("ctx_state", [128, 4 * 128], F32)
    krun = sbuf("krun", [128, 4], F32)
    recips = sbuf("recips", [128, 8], F32)
    sm_e = [sbuf(f"sm_e{i}", [128, 512], F32) for i in range(2)]
    sm_m = sbuf("sm_m", [128, 32], F32)
    sm_s = sbuf("sm_s", [128, 32], F32)
    sm_r = sbuf("sm_r", [128, 32], F32)
    katT = [sbuf(f"katT{i}", [128, 256], BF16) for i in range(4)]
    wres = [sbuf(f"wres{i}", [128, 512], F32) for i in range(3)]
    chat = [sbuf(f"chat{i}", [128, 128], BF16) for i in range(2)]
    krec_tmp = sbuf("krec_tmp", [128, 4], F32)
    qsm = [sbuf(f"qsm{i}", [128, 512], BF16) for i in range(2)]
    h2b = [sbuf(f"h2b{i}", [128, 8 * 256], BF16) for i in range(2)]
    gw_sb = sbuf("gw_sb", [128, 32], BF16)
    gb_row = sbuf("gb_row", [1, 4], BF16)
    gwt = sbuf("gwt", [128, 64], F32)
    gsc = sbuf("gsc", [128, 8], F32)
    scg = sbuf("scg", [128, 64], F32)
    ident = sbuf("ident_sb", [128, 128], BF16)
    masks = sbuf("masks_sb", [128, 512], BF16)
    ones_bf = sbuf("ones_bf", [128, 128], BF16)
    eps_col = sbuf("eps_col", [128, 1], F32)
    g1c = sbuf("g1c", [128, 8], F32)
    b1c = sbuf("b1c", [128, 8], F32)
    g2c = sbuf("g2c", [128, 8], F32)
    b2c = sbuf("b2c", [128, 8], F32)
    grow = sbuf("grow", [1, 4 * D], BF16)
    borow = sbuf("borow", [1, D], BF16)
    b1row = sbuf("b1row", [1, FF], BF16)
    b1cc = sbuf("b1cc", [128, 64], F32)
    b2row = sbuf("b2row", [1, D], BF16)
    ut = sbuf("ut", [128, 16 * 256], BF16)
    AR = 32 * 1024
    arena = sbuf("arena", [128, AR], BF16)

    mmA = psum("mmA", [128, 512], F32)
    mmB = psum("mmB", [128, 512], F32)
    trA = psum("trA", [128, 1024], BF16)
    trB = psum("trB", [128, 1024], BF16)
    dA = psum("dA", [128, 512], F32)
    dB = psum("dB", [128, 512], F32)
    kcum = psum("kcum", [128, 512], F32)
    spare = psum("spare", [128, 512], F32)

    c_pe = chan("c_pe")
    c_dv = chan("c_dv")
    c_ac = chan("c_ac")
    c_ldw = DmaChan([chan(f"c_ldw{i}", 16) for i in range(DmaChan.R)])
    c_ldx = DmaChan([chan(f"c_ldx{i}", 16) for i in range(DmaChan.R)])
    c_st = DmaChan([chan(f"c_st{i}", 16) for i in range(DmaChan.R)])
    c_gp = chan("c_gp")

    E = b.emit

    E("sync", lambda e: e.dma_start(out=ident[:, :], in_=ident_h), inc=c_ldx)
    E("sync", lambda e: e.dma_start(out=masks[:, :], in_=masks_h), inc=c_ldx)
    E("gpsimd", lambda e: e.memset(ones_bf[:, :], 1.0), inc=c_gp)
    E("gpsimd", lambda e: e.memset(eps_col[:, :], EPS), inc=c_gp)
    consts_ldx = c_ldx.at()
    consts_gp = c_gp.at()

    free = {}
    def buf_wait(name):
        return free.get(name, ())
    def buf_set(name, *pairs):
        free[name] = list(pairs)

    bank_free = {id(p): [] for p in (mmA, mmB, trA, trB, dA, dB, kcum, spare)}
    def bfree(p):
        return bank_free[id(p)]
    def bset(p, *pairs):
        bank_free[id(p)] = list(pairs)

    guards = {"rows_pe": 0, "borow_pe": 0, "b2row_pe": 0, "b1cc_ac": 0}

    def load_layer_smalls(L):
        E("gpsimd", lambda e: e.dma_start(out=grow[:, 0:D], in_=g1_h[L:L + 1, :]),
          waits=[(c_pe, guards["rows_pe"])], inc=c_ldw)
        E("gpsimd", lambda e: e.dma_start(out=grow[:, D:2 * D], in_=b1_h[L:L + 1, :]), inc=c_ldw)
        E("gpsimd", lambda e: e.dma_start(out=grow[:, 2 * D:3 * D], in_=g2_h[L:L + 1, :]), inc=c_ldw)
        E("gpsimd", lambda e: e.dma_start(out=grow[:, 3 * D:4 * D], in_=b2ln_h[L:L + 1, :]), inc=c_ldw)
        E("gpsimd", lambda e: e.dma_start(out=borow[:, :], in_=bo_h[L:L + 1, :]),
          waits=[(c_pe, guards["borow_pe"])], inc=c_ldw)
        E("gpsimd", lambda e: e.dma_start(out=b1row[:, :], in_=bb1_h[L:L + 1, :]), inc=c_ldw)
        E("gpsimd", lambda e: e.dma_start(out=b2row[:, :], in_=bb2_h[L:L + 1, :]),
          waits=[(c_pe, guards["b2row_pe"])], inc=c_ldw)
        w = c_ldw.at()

        def tr_rows(e):
            ins = None
            for j in range(4):
                for c in range(8):
                    ins = e.matmul(spare[:, j * 8 + c:j * 8 + c + 1],
                                   grow[0:1, j * D + c * 128:j * D + (c + 1) * 128],
                                   ones_bf[0:1, 0:1],
                                   start=(j == 0 and c == 0), stop=False)
            for c in range(32):
                ins = e.matmul(spare[:, 32 + c:33 + c],
                               b1row[0:1, c * 128:(c + 1) * 128],
                               ones_bf[0:1, 0:1], start=False, stop=(c == 31))
            return ins
        E("tensor", tr_rows, waits=[w, (c_gp, consts_gp)] + list(bfree(spare)), inc=c_pe)
        pe = c_pe.at()
        guards["rows_pe"] = pe
        E("vector", lambda e: e.tensor_copy(g1c[:, :], spare[:, 0:8]), waits=[(c_pe, pe)])
        E("vector", lambda e: e.tensor_copy(b1c[:, :], spare[:, 8:16]))
        E("vector", lambda e: e.tensor_copy(g2c[:, :], spare[:, 16:24]))
        E("vector", lambda e: e.tensor_copy(b2c[:, :], spare[:, 24:32]))
        E("vector", lambda e: e.tensor_copy(b1cc[:, 0:32], spare[:, 32:64]),
          waits=[(c_ac, guards["b1cc_ac"])], inc=c_dv)
        bset(spare, (c_dv, c_dv.at()))
        return c_dv.at()

    st2_act_read = [0] * PT
    rstd_dv_read = [0] * PT

    def tile_norm_stats(specs, extra_waits=()):
        v2s = {}
        for i, src in specs:
            E("vector", lambda e, i=i, src=src: e.bn_stats(st6[i][:, 0:6], src[:, 0:512]),
              waits=extra_waits)
            E("vector", lambda e, i=i, src=src: e.bn_stats(st6[i][:, 6:12], src[:, 512:1024]),
              inc=c_dv)
            v1 = c_dv.at()
            E("vector", lambda e, i=i: e.bn_aggr(st2[i][:, :], st6[i][:, :]),
              waits=[(c_dv, v1), (c_ac, st2_act_read[i])], inc=c_dv)
            v2s[i] = c_dv.at()
        a1s = {}
        for i, _ in specs:
            E("scalar", lambda e, i=i: e.activation(tinyA[:, i:i + 1], st2[i][:, 1:2], AF.Ln,
                                                    bias=eps_col[:, 0:1]),
              waits=[(c_dv, v2s[i]), (c_gp, consts_gp)], inc=c_ac)
            a_ln = c_ac.at()
            E("scalar", lambda e, i=i: e.activation(rstd[i][:, :], tinyA[:, i:i + 1],
                                                    AF.Exp, scale=-0.5),
              waits=[(c_dv, rstd_dv_read[i]), (c_ac, a_ln)], inc=c_ac)
            a1s[i] = c_ac.at()
            st2_act_read[i] = a1s[i]
        return a1s

    def tile_norm_apply(i, src, dsts, a1, extra_waits=()):
        for dst in dsts:
            E("vector", lambda e, dst=dst: e.tensor_scalar(
                dst[:, :], src[:, :], st2[i][:, 0:1], rstd[i][:, 0:1], ALU.subtract, ALU.mult),
              waits=[(c_ac, a1)] + list(extra_waits), inc=c_dv)
        rstd_dv_read[i] = c_dv.at()
        return c_dv.at()

    def do_transposes(src_fn, src_waits, dst_fn, affine, dst_busy=()):
        for grp in range(4):
            bank = trA if grp % 2 == 0 else trB
            def tr(e, grp=grp, bank=bank):
                ins = None
                for cc in range(2):
                    c = grp * 2 + cc
                    for i in range(4):
                        ins = e.transpose(bank[:, (cc * 4 + i) * 128:(cc * 4 + i + 1) * 128],
                                          src_fn(i, c), ident[:, :])
                return ins
            E("tensor", tr, waits=list(src_waits) + [consts_ldx] + list(bfree(bank)),
              inc=c_pe)
            pe = c_pe.at()
            for cc in range(2):
                c = grp * 2 + cc
                dap = dst_fn(c)
                if affine is None:
                    E("scalar", lambda e, cc=cc, bank=bank, dap=dap: e.activation(
                        dap, bank[:, cc * 512:(cc + 1) * 512], AF.Copy),
                      waits=[(c_pe, pe), dst_busy], inc=c_ac)
                else:
                    g, bb = affine
                    E("vector", lambda e, c=c, cc=cc, bank=bank, dap=dap, g=g, bb=bb:
                      e.tensor_scalar(dap, bank[:, cc * 512:(cc + 1) * 512],
                                      g[:, c:c + 1], bb[:, c:c + 1], ALU.mult, ALU.add),
                      waits=[(c_pe, pe), dst_busy], inc=c_dv)
            if affine is None:
                bset(bank, (c_ac, c_ac.at()))
            else:
                bset(bank, (c_dv, c_dv.at()))
        return [(c_ac, c_ac.at()), (c_dv, c_dv.at())]

    def prologue_prime(L, p):
        tiles = list(range(p * PT, (p + 1) * PT))
        at_busy = buf_wait("at")
        hraw_busy = buf_wait("hraw")
        if L == 0:
            for i, t in enumerate(tiles):
                E("sync", lambda e, t=t, i=i: e.dma_start(
                    out=at[i][:, :], in_=x_h[t * 128:(t + 1) * 128, :]),
                  waits=[at_busy], inc=c_ldx)
            lx = c_ldx.at()
            a1s = tile_norm_stats([(i, at[i]) for i in range(PT)], extra_waits=[lx])
            for i, t in enumerate(tiles):
                tile_norm_apply(i, at[i], [at[i]], a1s[i])
                E("vector", lambda e, i=i: e.tensor_copy(hraw[i][:, :], at[i][:, :]),
                  waits=[hraw_busy, (c_dv, c_dv.at())], inc=c_dv)
                xn_done = c_dv.at()
                E("sync", lambda e, t=t, i=i: e.dma_start(
                    out=a_hbm[t * 128:(t + 1) * 128, :], in_=at[i][:, :]),
                  waits=[(c_dv, xn_done)], inc=c_st)
                buf_set(f"a_hbm_t{t}", *c_st.at())
                E("sync", lambda e, t=t, i=i: e.dma_start(
                    out=xn_hbm[t * 128:(t + 1) * 128, :], in_=hraw[i][:, :]),
                  waits=[(c_dv, xn_done)], inc=c_st)
                buf_set(f"xn_hbm_t{t}", *c_st.at())
            spill_w = c_st.at()
            buf_set("hraw", *(list(buf_wait("hraw")) + list(spill_w)))
            a1s = tile_norm_stats([(i, at[i]) for i in range(PT)])
            return tiles, a1s, [hraw_busy, spill_w]
        for i, t in enumerate(tiles):
            E("sync", lambda e, t=t, i=i: e.dma_start(
                out=at[i][:, :], in_=a_hbm[t * 128:(t + 1) * 128, :]),
              waits=[at_busy, buf_wait(f"a_hbm_t{t}")], inc=c_ldx)
        lx = c_ldx.at()
        a1s = tile_norm_stats([(i, at[i]) for i in range(PT)], extra_waits=[lx])
        return tiles, a1s, [hraw_busy]

    def prologue_finish(L, p, prime):
        tiles, a1s, extraw = prime
        h_done = 0
        for i in range(PT):
            h_done = tile_norm_apply(i, at[i], [hraw[i]], a1s[i], extra_waits=extraw)
        vr = do_transposes(lambda i, c: hraw[i][:, c * 128:(c + 1) * 128],
                           [(c_dv, h_done)], lambda c: h1T[:, c * 512:(c + 1) * 512],
                           (g1c, b1c), dst_busy=rtake("h1T"))
        rclear("h1T")
        buf_set("hraw", *vr)
        buf_set("at", (c_dv, h_done), *c_st.at())
        if dbg_h1T is not None and L == 0:
            E("sync", lambda e, p=p: e.dma_start(
                out=dbg_h1T[:, :, p * 512:(p + 1) * 512],
                in_=h1T[:, :].rearrange("q (c n) -> q c n", c=8)),
              waits=vr, inc=c_st)
            raddl("h1T", c_st.at())
        return vr

    def emit_prologue(L, p):
        return prologue_finish(L, p, prologue_prime(L, p))

    SL_WQ, SL_WK, SL_WV, SL_WO = 0, 8192, 16384, 24576
    SL_W1H, SL_W2H = 0, 16384
    SCL = float(DH) ** -0.5

    def load_wchunks(dst_col, src_h, rows0, rows1, cols0, cols1, waits=(), stages=None):
        ncols = cols1 - cols0
        st = {}
        for ci, r in enumerate(range(rows0, rows1, 128)):
            ap_in = src_h[r:r + 128, cols0:cols1]
            ap_out = arena[:, dst_col + ci * ncols: dst_col + (ci + 1) * ncols]
            E("gpsimd", lambda e, ap_out=ap_out, ap_in=ap_in: e.dma_start(
                out=ap_out, in_=ap_in), waits=waits, inc=c_ldw)
            waits = ()
            st[ci] = c_ldw.at()
        if stages is not None:
            return st
        return c_ldw.at()

    def mm_group(out_ap, pairs, waits, rank1=None, start_first=True, stop_last=True):
        pairs = tuple(pairs)
        waits = list(waits) + [consts_ldx, (c_gp, consts_gp)]

        def go(e, out_ap=out_ap, pairs=pairs, rank1=rank1):
            ins = None
            last = len(pairs) - 1
            for idx, (l, r) in enumerate(pairs):
                ins = e.matmul(out_ap, l, r, start=(idx == 0 and start_first),
                               stop=(stop_last and idx == last and rank1 is None))
            if rank1 is not None:
                l, r = rank1
                ins = e.matmul(out_ap, l, r, start=False, stop=stop_last)
            return ins
        E("tensor", go, waits=waits, inc=c_pe)
        return c_pe.at()

    rdmap = {}

    def radd(name, ch, v):
        lst = rdmap.setdefault(name, [])
        for i2, (c0, v0) in enumerate(lst):
            if c0 is ch:
                lst[i2] = (ch, max(v0, v))
                return
        lst.append((ch, v))

    def rtake(name):
        return list(rdmap.get(name, []))

    def rclear(name):
        rdmap[name] = []

    def raddl(name, pairs):
        for ch, v in pairs:
            radd(name, ch, v)

    def slot(t):
        return t % NVSLOT

    vready = {}
    h2T_spilled = {}
    mmrot = [0]
    mm_pool = [[mmA, mmB]]

    def set_mm_pool(banks):
        mm_pool[0] = list(banks)

    def next_mm():
        pool = mm_pool[0]
        bank = pool[mmrot[0] % len(pool)]
        mmrot[0] += 1
        return bank

    sm_guard = {}
    sme_guard = [[], []]
    katT_guard = [[], [], [], []]
    chat_guard = [[], []]
    qsm_guard = [[], []]
    h2b_guard = [[], []]

    def emit_qkv(L, p, h1T_ready, wts, interleave=None):
        def il(n=1):
            if interleave is not None:
                for _ in range(n):
                    interleave()
        wq_t, wk_t, wv_t, wo_t = wts
        res = {}
        if p == 0:
            res["kD_w"] = rtake("kDloc")
            rclear("kDloc")
            emit_qkv.kd_w = res["kD_w"]
        for i in range(PT):
            t = p * PT + i
            sl = slot(t)
            v_w = rtake(f"v{sl}")
            rclear(f"v{sl}")
            for dh in range(2):
                bank = next_mm()
                pairs = [(h1T[:, c * 512 + i * 128: c * 512 + (i + 1) * 128],
                          arena[:, SL_WV + c * 1024 + dh * 512: SL_WV + c * 1024 + (dh + 1) * 512])
                         for c in range(8)]
                pe = mm_group(bank[:, :], pairs,
                              waits=[h1T_ready, wv_t] + bfree(bank))
                dst = vtok[sl][:, dh * 512:(dh + 1) * 512]
                E("scalar", lambda e, bank=bank, dst=dst: e.activation(dst, bank[:, :], AF.Copy),
                  waits=[(c_pe, pe)] + v_w, inc=c_ac)
                bset(bank, (c_ac, c_ac.at()))
                il(2)
            vready[t] = c_ac.at()
        res["v"] = c_ac.at()
        qd_w = rtake("qDloc")
        rclear("qDloc")
        for m in range(4):
            bank = next_mm()
            pairs = [(arena[:, SL_WQ + c * 1024 + m * 128: SL_WQ + c * 1024 + (m + 1) * 128],
                      h1T[:, c * 512:(c + 1) * 512]) for c in range(8)]
            pe = mm_group(bank[:, :], pairs,
                          waits=[h1T_ready, wq_t] + bfree(bank))
            dst = qDloc[:, m * 512:(m + 1) * 512]
            E("scalar", lambda e, bank=bank, dst=dst: e.activation(
                dst, bank[:, :], AF.Copy, scale=SCL),
              waits=[(c_pe, pe)] + qd_w, inc=c_ac)
            bset(bank, (c_ac, c_ac.at()))
            il()
        res["qDloc"] = c_ac.at()
        for m in range(4):
            bank = next_mm()
            pairs = [(arena[:, SL_WK + c * 1024 + m * 128: SL_WK + c * 1024 + (m + 1) * 128],
                      h1T[:, c * 512:(c + 1) * 512]) for c in range(8)]
            pe = mm_group(bank[:, :], pairs,
                          waits=[h1T_ready, wk_t] + bfree(bank))
            dst = kDloc[:, m * T + p * 512: m * T + (p + 1) * 512]
            E("scalar", lambda e, bank=bank, dst=dst: e.activation(dst, bank[:, :], AF.Copy),
              waits=[(c_pe, pe)] + emit_qkv.kd_w, inc=c_ac)
            bset(bank, (c_ac, c_ac.at()))
            il()
        res["kDloc"] = c_ac.at()
        qdl_w = rtake("qDlin")
        rclear("qDlin")
        pend_trq = []

        def defer_trq(i, par, vq):
            def doit(i=i, par=par, vq=vq):
                bank_t = (trA, trB)[par]

                def trq(e, par=par, bank_t=bank_t):
                    ins = None
                    for h in range(4):
                        ins = e.transpose(bank_t[:, h * 128:(h + 1) * 128],
                                          qsm[par][:, h * 128:(h + 1) * 128], ident[:, :])
                    return ins
                E("tensor", trq, waits=[(c_ac, vq)] + bfree(bank_t), inc=c_pe)
                pe_t = c_pe.at()
                qsm_guard[par] = [(c_pe, pe_t)]
                dst = qDlin[:, i * 512:(i + 1) * 512]
                E("vector", lambda e, bank_t=bank_t, dst=dst: e.tensor_copy(
                    dst, bank_t[:, 0:512]), waits=[(c_pe, pe_t)] + qdl_w, inc=c_dv)
                bset(bank_t, (c_dv, c_dv.at()))
            pend_trq.append(doit)

        def pump_trq():
            if pend_trq:
                pend_trq.pop(0)()

        for i in range(PT):
            if i >= 2:
                pump_trq()
            bank = next_mm()
            pairs = [(h1T[:, c * 512 + i * 128: c * 512 + (i + 1) * 128],
                      arena[:, SL_WQ + c * 1024 + 512: SL_WQ + c * 1024 + 1024])
                     for c in range(8)]
            pe = mm_group(bank[:, :], pairs,
                          waits=[h1T_ready, wq_t] + bfree(bank))
            par = i % 2
            for h in range(4):
                col = i * 4 + h
                E("vector", lambda e, bank=bank, h=h, col=col: e.reduce_max(
                    sm_m[:, col:col + 1], bank[:, h * 128:(h + 1) * 128], axis=AX, negate=True),
                  waits=[(c_pe, pe)] + sm_guard.get(col, []), inc=c_dv)
            v_m = c_dv.at()
            for h in range(4):
                col = i * 4 + h
                E("scalar", lambda e, bank=bank, h=h, col=col, par=par: e.activation(
                    sm_e[par][:, h * 128:(h + 1) * 128], bank[:, h * 128:(h + 1) * 128],
                    AF.Exp, bias=sm_m[:, col:col + 1], accum_out=sm_s[:, col:col + 1]),
                  waits=[(c_dv, v_m)] + sme_guard[par], inc=c_ac)
            a_e = c_ac.at()
            bset(bank, (c_ac, a_e))
            E("vector", lambda e, i=i: e.reciprocal(
                sm_r[:, i * 4:(i + 1) * 4], sm_s[:, i * 4:(i + 1) * 4]),
              waits=[(c_ac, a_e)], inc=c_dv)
            v_r = c_dv.at()
            for h in range(4):
                col = i * 4 + h
                E("scalar", lambda e, h=h, col=col, par=par: e.activation(
                    qsm[par][:, h * 128:(h + 1) * 128], sm_e[par][:, h * 128:(h + 1) * 128],
                    AF.Copy, scale=sm_r[:, col:col + 1]),
                  waits=[(c_dv, v_r)] + qsm_guard[par], inc=c_ac)
            vq = c_ac.at()
            sme_guard[par] = [(c_ac, vq)]
            for h in range(4):
                sm_guard[i * 4 + h] = [(c_ac, vq)]
            defer_trq(i, par, vq)
            il()
        pump_trq()
        for i in range(PT):
            if i == 1:
                pump_trq()
            bank = next_mm()
            pairs = [(h1T[:, c * 512 + i * 128: c * 512 + (i + 1) * 128],
                      arena[:, SL_WK + c * 1024 + 512: SL_WK + c * 1024 + 1024])
                     for c in range(8)]
            pe = mm_group(bank[:, :], pairs,
                          waits=[h1T_ready, wk_t] + bfree(bank))
            par = i % 2
            E("vector", lambda e, bank=bank, par=par: e.tensor_scalar(
                qsm[par][:, :], bank[:, :], -1.0, 0.0, ALU.mult, ALU.max),
              waits=[(c_pe, pe)] + qsm_guard[par], inc=c_dv)
            v_relu = c_dv.at()
            E("scalar", lambda e, par=par: e.activation(
                sm_e[par][:, :], qsm[par][:, :], AF.Exp, scale=-1.0),
              waits=[(c_dv, v_relu)] + sme_guard[par], inc=c_ac)
            ae = c_ac.at()
            qsm_guard[par] = [(c_ac, ae)]
            kl_w = rtake(f"klin{i}")
            rclear(f"klin{i}")
            E("vector", lambda e, bank=bank, i=i, par=par: e.scalar_tensor_tensor(
                klin[i][:, :], bank[:, :], 0.0, sm_e[par][:, :], ALU.max, ALU.add),
              waits=[(c_ac, ae)] + kl_w, inc=c_dv)
            vk = c_dv.at()
            sme_guard[par] = [(c_dv, vk)]
            bset(bank, (c_dv, vk))
            il()
        res["klin"] = c_dv.at()
        res["qDlin"] = res["klin"]
        res["last_pe"] = c_pe.at()
        radd("h1T", c_pe, res["last_pe"])
        return res

    lin_state = {"rec_tgt": {}, "state_tgt": 0, "kcum_read": 0, "rec_read": {},
                 "state_zero": 0, "kcum_started": False}

    class AttnStepper:

        def __init__(self, L, p, qkv, ao_w, ut_w):
            self.L, self.p, self.qkv, self.ao_w, self.ut_w = L, p, qkv, ao_w, ut_w
            self.ci = 0
            self.bi = 0
            self.chain_t = {}
            self.dpool = [dA, dB, spare]

        def step_chain(self):
            if self.ci >= 16:
                return False
            n = self.ci
            self.ci += 1
            self._attn_item(n // 4, n % 4)
            return True

        def step_bucket(self):
            if self.bi >= 16:
                return False
            n = self.bi
            self.bi += 1
            if n % 2 == 0:
                self._lin_bucket_a(n // 2)
            else:
                self._lin_bucket_b(n // 2)
            return True

        def step(self):
            if self.step_chain():
                return True
            if self.step_bucket():
                self.step_bucket()
                return True
            return False

        def drain(self):
            while self.step():
                pass

        def _attn_item(self, i, h):
            L, p, qkv = self.L, self.p, self.qkv
            wb = p * PT + i
            sl4 = i * 4 + h
            dbank = self.dpool[sl4 % 3]
            col = 16 + sl4
            par = sl4 % 2
            mask_ap = masks[:, 0:256] if wb == 0 else masks[:, 256:512]
            q_ap = qDloc[:, h * 512 + i * 128: h * 512 + (i + 1) * 128]
            if wb == 0:
                k_ap = kDloc[:, h * T: h * T + 128]
            else:
                k_ap = kDloc[:, h * T + (wb - 1) * 128: h * T + (wb + 1) * 128]

            def dots(e, dbank=dbank, mask_ap=mask_ap, q_ap=q_ap, k_ap=k_ap, wb=wb):
                e.matmul(dbank[:, 0:256], ident[:, :], mask_ap, start=True, stop=False)
                if wb == 0:
                    return e.matmul(dbank[:, 128:256], q_ap, k_ap, start=False, stop=True)
                return e.matmul(dbank[:, 0:256], q_ap, k_ap, start=False, stop=True)
            E("tensor", dots,
              waits=[(c_ac, qkv["qDloc"]), (c_ac, qkv["kDloc"]),
                     consts_ldx, (c_gp, consts_gp)] + bfree(dbank), inc=c_pe)
            pe_d = c_pe.at()
            radd("qDloc", c_pe, pe_d)
            radd("kDloc", c_pe, pe_d)
            E("vector", lambda e, dbank=dbank, col=col: e.reduce_max(
                sm_m[:, col:col + 1], dbank[:, 0:256], axis=AX, negate=True),
              waits=[(c_pe, pe_d)] + sm_guard.get(col, []), inc=c_dv)
            v_m = c_dv.at()
            E("scalar", lambda e, dbank=dbank, col=col, par=par: e.activation(
                sm_e[par][:, 0:256], dbank[:, 0:256], AF.Exp,
                bias=sm_m[:, col:col + 1], accum_out=sm_s[:, col:col + 1]),
              waits=[(c_dv, v_m)] + sme_guard[par], inc=c_ac)
            a_e = c_ac.at()
            bset(dbank, (c_ac, a_e))
            E("vector", lambda e, col=col: e.reciprocal(
                sm_r[:, col:col + 1], sm_s[:, col:col + 1]),
              waits=[(c_ac, a_e)], inc=c_dv)
            v_r = c_dv.at()
            dst = ut[:, sl4 * 256:(sl4 + 1) * 256]
            E("scalar", lambda e, col=col, par=par, dst=dst: e.activation(
                dst, sm_e[par][:, 0:256], AF.Copy, scale=sm_r[:, col:col + 1]),
              waits=[(c_dv, v_r)] + self.ut_w, inc=c_ac)
            v_n = c_ac.at()
            sme_guard[par] = [(c_ac, v_n)]
            sm_guard[col] = [(c_ac, v_n)]
            self.chain_t[(i, h)] = v_n

        def _lin_bucket(self, j):
            self._lin_bucket_a(j)
            self._lin_bucket_b(j)

        def _lin_bucket_a(self, j):
            L, p, qkv = self.L, self.p, self.qkv
            ao_w = rtake("aoT")
            u = p * 8 + j
            i = j // 2
            half = j % 2
            t = p * PT + i
            for h in range(4):
                if u == 0:
                    dst = aoT[:, (4 + h) * 512: (4 + h) * 512 + 64]
                    E("gpsimd", lambda e, dst=dst: e.memset(dst, 0.0),
                      waits=ao_w, inc=c_gp)
                else:
                    par = (j * 4 + h) % 2
                    rcol = ((u - 1) % 2) * 4 + h
                    E("vector", lambda e, h=h, par=par, rcol=rcol: e.tensor_scalar_mul(
                        chat[par][:, :], ctx_state[:, h * 128:(h + 1) * 128],
                        recips[:, rcol:rcol + 1]),
                      waits=[(c_dv, lin_state["rec_tgt"].get(u - 1, 0)),
                             (c_gp, lin_state["state_zero"])] + chat_guard[par], inc=c_dv)
                    vc = c_dv.at()
                    lin_state["rec_read"][u - 1] = vc
                    obank = next_mm()
                    q_ap = qDlin[:, i * 512 + h * 128 + half * 64:
                                 i * 512 + h * 128 + half * 64 + 64]
                    E("tensor", lambda e, obank=obank, par=par, q_ap=q_ap: e.matmul(
                        obank[:, 0:64], chat[par][:, :], q_ap, start=True, stop=True),
                      waits=[(c_dv, vc), (c_dv, qkv["qDlin"])] + bfree(obank), inc=c_pe)
                    pe_o = c_pe.at()
                    chat_guard[par] = [(c_pe, pe_o)]
                    radd("qDlin", c_pe, pe_o)
                    dst = aoT[:, (4 + h) * 512 + j * 64: (4 + h) * 512 + (j + 1) * 64]
                    E("scalar", lambda e, obank=obank, dst=dst: e.activation(
                        dst, obank[:, 0:64], AF.Copy),
                      waits=[(c_pe, pe_o)] + ao_w, inc=c_ac)
                    bset(obank, (c_ac, c_ac.at()))

        def _lin_bucket_b(self, j):
            L, p, qkv = self.L, self.p, self.qkv
            u = p * 8 + j
            i = j // 2
            half = j % 2
            t = p * PT + i
            base = half * 64
            pe_ks = 0
            for h in range(4):
                cbank = (dA, dB)[h % 2]
                k_ap = klin[i][base:base + 64, h * 128:(h + 1) * 128]
                v_ap = vtok[slot(t)][base:base + 64, (4 + h) * 128:(5 + h) * 128]
                o_ap = ones_bf[base:base + 64, 0:1]

                def cmm(e, cbank=cbank, k_ap=k_ap, v_ap=v_ap, o_ap=o_ap, h=h):
                    e.matmul(cbank[:, 0:128], k_ap, v_ap, start=True, stop=True)
                    return e.matmul(kcum[:, h:h + 1], k_ap, o_ap, start=True,
                                    stop=True)
                E("tensor", cmm,
                  waits=[(c_dv, qkv["klin"]), (c_ac, vready[t]), (c_gp, consts_gp),
                         (c_dv, lin_state["kcum_read"])] + bfree(cbank) + bfree(kcum),
                  inc=c_pe)
                pe_c = c_pe.at()
                pe_ks = pe_c
                radd(f"klin{i}", c_pe, pe_c)
                radd(f"v{slot(t)}", c_pe, pe_c)
                E("vector", lambda e, cbank=cbank, h=h: e.scalar_tensor_tensor(
                    ctx_state[:, h * 128:(h + 1) * 128], cbank[:, 0:128], 1.0,
                    ctx_state[:, h * 128:(h + 1) * 128], ALU.mult, ALU.add),
                  waits=[(c_pe, pe_c), (c_gp, lin_state["state_zero"])], inc=c_dv)
                bset(cbank, (c_dv, c_dv.at()))
            lin_state["state_tgt"] = c_dv.at()
            rcol = (u % 2) * 4
            E("vector", lambda e: e.scalar_tensor_tensor(
                krun[:, :], kcum[:, 0:4], 1.0, krun[:, :], ALU.mult, ALU.add),
              waits=[(c_pe, pe_ks), (c_gp, lin_state["state_zero"])], inc=c_dv)
            lin_state["kcum_read"] = c_dv.at()
            E("vector", lambda e: e.tensor_scalar_add(krec_tmp[:, :], krun[:, :], 1e-6),
              waits=[(c_dv, lin_state["kcum_read"]),
                     (c_dv, lin_state["rec_read"].get(u - 1, 0))], inc=c_dv)
            E("vector", lambda e, rcol=rcol: e.reciprocal(
                recips[:, rcol:rcol + 4], krec_tmp[:, :]),
              waits=[(c_dv, c_dv.at())], inc=c_dv)
            lin_state["rec_tgt"][u] = c_dv.at()

    def emit_attnT_out(L, p, stepper, ao_w):
        items = [(i, h) for i in range(PT) for h in range(4)]

        def emit_tr(k):
            i, h = items[k]
            sl4 = i * 4 + h
            tbank = (trA, trB)[sl4 % 2]
            v_n = stepper.chain_t[(i, h)]
            base_c = sl4 * 256

            def trat(e, tbank=tbank, base_c=base_c):
                e.transpose(tbank[:, 0:128], ut[:, base_c:base_c + 128], ident[:, :])
                return e.transpose(tbank[:, 128:256],
                                   ut[:, base_c + 128:base_c + 256], ident[:, :])
            E("tensor", trat, waits=[(c_ac, v_n)] + bfree(tbank), inc=c_pe)
            pe_t = c_pe.at()
            radd("ut", c_pe, pe_t)
            return tbank, pe_t

        def emit_out(k, tbank, pe_t):
            i, h = items[k]
            sl4 = i * 4 + h
            par4 = sl4 % 4
            wb = p * PT + i
            E("vector", lambda e, tbank=tbank, par4=par4: e.tensor_copy(
                katT[par4][:, :], tbank[:, 0:256]),
              waits=[(c_pe, pe_t)] + katT_guard[par4], inc=c_dv)
            v_t = c_dv.at()
            bset(tbank, (c_dv, v_t))
            obank = next_mm()

            def outmm(e, obank=obank, par4=par4, wb=wb, h=h):
                ins = None
                first = True
                for jc in (0, 1):
                    kb = wb - 1 + jc
                    if kb < 0:
                        continue
                    ins = e.matmul(obank[:, 0:128],
                                   vtok[slot(kb)][:, h * 128:(h + 1) * 128],
                                   katT[par4][:, jc * 128:(jc + 1) * 128],
                                   start=first, stop=(jc == 1))
                    first = False
                return ins
            w_v = [(c_ac, vready[wb])]
            if wb > 0:
                w_v.append((c_ac, vready[wb - 1]))
            E("tensor", outmm, waits=[(c_dv, v_t)] + w_v + bfree(obank), inc=c_pe)
            pe_o = c_pe.at()
            katT_guard[par4] = [(c_pe, pe_o)]
            radd(f"v{slot(wb)}", c_pe, pe_o)
            if wb > 0:
                radd(f"v{slot(wb - 1)}", c_pe, pe_o)
            dst = aoT[:, h * 512 + i * 128: h * 512 + (i + 1) * 128]
            E("scalar", lambda e, obank=obank, dst=dst: e.activation(
                dst, obank[:, 0:128], AF.Copy),
              waits=[(c_pe, pe_o)] + ao_w, inc=c_ac)
            bset(obank, (c_ac, c_ac.at()))

        pend = emit_tr(0)
        for k in range(1, 16):
            nxt = emit_tr(k)
            emit_out(k - 1, *pend)
            pend = nxt
        emit_out(15, *pend)
        return c_ac.at()

    wres_free = [[], [], []]
    st2w_rd = [[] for _ in range(PT)]
    rstdw_rd = [[] for _ in range(PT)]

    def emit_wo(L, p, ao_done, wo_t, interleave=None):
        hraw_busy = list(buf_wait("hraw"))
        h_done = 0
        for i in range(PT):
            t = p * PT + i
            wp = lambda dh: (i * 2 + dh) % 3
            lw = {}
            for dh in (0, 1):
                wb_buf = wres[wp(dh)]
                E("sync", lambda e, t=t, dh=dh, wb_buf=wb_buf: e.dma_start(
                    out=wb_buf[:, :],
                    in_=a_hbm[t * 128:(t + 1) * 128, dh * 512:(dh + 1) * 512]),
                  waits=wres_free[wp(dh)] + list(buf_wait(f"a_hbm_t{t}")), inc=c_ldx)
                lw[dh] = c_ldx.at()
            st_t = {}
            for dh in (0, 1):
                wb_buf = wres[wp(dh)]
                bank = next_mm()
                pairs = [(aoT[:, c * 512 + i * 128: c * 512 + (i + 1) * 128],
                          arena[:, SL_WO + c * 1024 + dh * 512: SL_WO + c * 1024 + (dh + 1) * 512])
                         for c in range(8)]
                rank1 = (ones_bf[0:1, 0:128], borow[0:1, dh * 512:(dh + 1) * 512])
                pe = mm_group(bank[:, :], pairs,
                              waits=ao_done + [wo_t] + bfree(bank), rank1=rank1)
                radd("aoT", c_pe, pe)
                if interleave is not None:
                    interleave()
                    interleave()
                E("vector", lambda e, bank=bank, wb_buf=wb_buf: e.scalar_tensor_tensor(
                    wb_buf[:, :], bank[:, :], 1.0, wb_buf[:, :], ALU.mult, ALU.add),
                  waits=[(c_pe, pe), lw[dh]], inc=c_dv)
                bset(bank, (c_dv, c_dv.at()))
                E("sync", lambda e, t=t, dh=dh, wb_buf=wb_buf: e.dma_start(
                    out=a_hbm[t * 128:(t + 1) * 128, dh * 512:(dh + 1) * 512],
                    in_=wb_buf[:, :]),
                  waits=[(c_dv, c_dv.at())], inc=c_st)
                st_t[dh] = c_st.at()
            buf_set(f"a_hbm_t{t}", *c_st.at())
            E("vector", lambda e, i=i, w0=wp(0): e.bn_stats(st6w[i][:, 0:6], wres[w0][:, :]))
            E("vector", lambda e, i=i, w1=wp(1): e.bn_stats(st6w[i][:, 6:12], wres[w1][:, :]),
              inc=c_dv)
            v1 = c_dv.at()
            E("vector", lambda e, i=i: e.bn_aggr(st2w[i][:, :], st6w[i][:, :]),
              waits=[(c_dv, v1)] + st2w_rd[i], inc=c_dv)
            v2 = c_dv.at()
            E("scalar", lambda e, i=i: e.activation(tinyW[:, i:i + 1], st2w[i][:, 1:2], AF.Ln,
                                                    bias=eps_col[:, 0:1]),
              waits=[(c_dv, v2)], inc=c_ac)
            a_ln = c_ac.at()
            E("scalar", lambda e, i=i: e.activation(rstdw[i][:, :], tinyW[:, i:i + 1],
                                                    AF.Exp, scale=-0.5),
              waits=rstdw_rd[i] + [(c_ac, a_ln)], inc=c_ac)
            a1 = c_ac.at()
            for dh in (0, 1):
                dst = hraw[i][:, dh * 512:(dh + 1) * 512]
                E("vector", lambda e, i=i, dh=dh, dst=dst, ws=wp(dh): e.tensor_scalar(
                    dst, wres[ws][:, :], st2w[i][:, 0:1], rstdw[i][:, 0:1],
                    ALU.subtract, ALU.mult),
                  waits=[(c_ac, a1)] + hraw_busy, inc=c_dv)
            h_done = c_dv.at()
            st2w_rd[i] = [(c_ac, a_ln), (c_dv, h_done)]
            rstdw_rd[i] = [(c_dv, h_done)]
            for dh in (0, 1):
                wres_free[wp(dh)] = [(c_dv, h_done)] + st_t[dh]
        buf_set("at", (c_dv, c_dv.at()), *c_st.at())
        vr = do_transposes(lambda i, c: hraw[i][:, c * 128:(c + 1) * 128],
                           [(c_dv, h_done)], lambda c: h1T[:, c * 512:(c + 1) * 512],
                           (g2c, b2c), dst_busy=rtake("h1T"))
        rclear("h1T")
        buf_set("hraw", *vr)
        E("sync", lambda e, p=p: e.dma_start(
            out=h2T_hbm[:, :, p * 512:(p + 1) * 512],
            in_=h1T[:, :].rearrange("q (c n) -> q c n", c=8)),
          waits=vr, inc=c_st)
        h2T_spilled[p] = c_st.at()
        raddl("h1T", h2T_spilled[p])

    SLQ = [(0, 8192), (16384, 24576)]

    def load_quarter(w1src, w2src, q, par, w1_waits, w2_waits):
        b1, b2 = SLQ[par]
        for ci in range(8):
            ap_in = w1src[ci * 128:(ci + 1) * 128, q * 1024:(q + 1) * 1024]
            ap_out = arena[:, b1 + ci * 1024: b1 + (ci + 1) * 1024]
            E("gpsimd", lambda e, ap_out=ap_out, ap_in=ap_in: e.dma_start(
                out=ap_out, in_=ap_in), waits=w1_waits, inc=c_ldw)
            w1_waits = ()
        t1 = c_ldw.at()
        for ci in range(8):
            ap_in = w2src[q * 1024 + ci * 128: q * 1024 + (ci + 1) * 128, 0:D]
            ap_out = arena[:, b2 + ci * 1024: b2 + (ci + 1) * 1024]
            E("gpsimd", lambda e, ap_out=ap_out, ap_in=ap_in: e.dma_start(
                out=ap_out, in_=ap_in), waits=w2_waits, inc=c_ldw)
            w2_waits = ()
        return t1, c_ldw.at()

    def emit_mlp(L, w1_guard, w2_guard):
        set_mm_pool([mmA, mmB, dA, dB, kcum])
        bset(kcum, (c_dv, lin_state["kcum_read"]),
             (c_dv, lin_state["rec_tgt"].get(31, 0)))
        wq_t = {}
        wq_t[0] = load_quarter(W1_h[L], W2_h[L], 0, 0, w1_guard, w1_guard)
        wq_t[1] = load_quarter(W1_h[L], W2_h[L], 1, 1, w2_guard, w2_guard)
        qlast = {}
        hb_tgt = {}
        for q in range(4):
            b1, b2 = SLQ[q % 2]
            w1_t, w2_t = wq_t[q]
            for sp in range(8):
                for spl in ([0, 1] if (sp == 0 and q == 0) else
                            ([(sp + 1) % 8] if not (q == 3 and sp == 7) else [])):
                    off2 = spl * 256
                    hb2 = h2b[spl % 2]
                    E("sync", lambda e, hb2=hb2, off2=off2: e.dma_start(
                        out=hb2[:, :].rearrange("q (c n) -> q c n", c=8),
                        in_=h2T_hbm[:, :, off2:off2 + 256]),
                      waits=h2T_spilled[spl // 2] + h2b_guard[spl % 2], inc=c_ldx)
                    hb_tgt[spl] = c_ldx.at()
                hb = h2b[sp % 2]
                lx = hb_tgt[sp]
                at_pre = {}
                for ii in (0, 1):
                    t = sp * 2 + ii
                    atb = at[2 * (sp % 2) + ii]
                    E("sync", lambda e, t=t, atb=atb: e.dma_start(
                        out=atb[:, :], in_=a_hbm[t * 128:(t + 1) * 128, :]),
                      waits=[buf_wait(f"a_hbm_t{t}"), buf_wait("at"),
                             buf_wait(f"atmlp{2 * (sp % 2) + ii}")],
                      inc=c_ldx)
                    at_pre[ii] = c_ldx.at()
                ut_w = rtake("ut")
                rclear("ut")
                for fc in range(8):
                    bank = next_mm()
                    pairs = [(arena[:, b1 + c * 1024 + fc * 128: b1 + c * 1024 + (fc + 1) * 128],
                              hb[:, c * 256:(c + 1) * 256]) for c in range(8)]
                    pe = mm_group(bank[:, 0:256], pairs,
                                  waits=[lx, w1_t] + bfree(bank))
                    h2b_guard[sp % 2] = [(c_pe, pe)]
                    bcol = b1cc[:, q * 8 + fc: q * 8 + fc + 1]
                    dst = ut[:, fc * 256:(fc + 1) * 256]
                    E("scalar", lambda e, bank=bank, dst=dst, bcol=bcol: e.activation(
                        dst, bank[:, 0:256], AF.Gelu, bias=bcol),
                      waits=[(c_pe, pe)] + ut_w, inc=c_ac)
                    bset(bank, (c_ac, c_ac.at()))
                ut_t = c_ac.at()
                last_w1_pe = c_pe.at()
                for ii in (0, 1):
                    t = sp * 2 + ii
                    atb = at[2 * (sp % 2) + ii]
                    lxa = at_pre[ii]
                    for dh in (0, 1):
                        bank = next_mm()
                        pairsA = [(ut[:, fc * 256 + ii * 128: fc * 256 + ii * 128 + 128],
                                   arena[:, b2 + fc * 1024 + dh * 512: b2 + fc * 1024 + (dh + 1) * 512])
                                  for fc in range(8)]
                        rank1 = (ones_bf[0:1, 0:128], b2row[0:1, dh * 512:(dh + 1) * 512]) \
                            if q == 0 else None
                        pe = mm_group(bank[:, :], pairsA,
                                      waits=[(c_ac, ut_t), w2_t] + bfree(bank),
                                      rank1=rank1)
                        radd("ut", c_pe, pe)
                        E("vector", lambda e, bank=bank, atb=atb, dh=dh: e.scalar_tensor_tensor(
                            atb[:, dh * 512:(dh + 1) * 512], bank[:, :], 1.0,
                            atb[:, dh * 512:(dh + 1) * 512], ALU.mult, ALU.add),
                          waits=[(c_pe, pe), lxa], inc=c_dv)
                        bset(bank, (c_dv, c_dv.at()))
                    vd = c_dv.at()
                    E("sync", lambda e, t=t, atb=atb: e.dma_start(
                        out=a_hbm[t * 128:(t + 1) * 128, :], in_=atb[:, :]),
                      waits=[(c_dv, vd)], inc=c_st)
                    buf_set(f"a_hbm_t{t}", *c_st.at())
                    buf_set(f"atmlp{2 * (sp % 2) + ii}", (c_dv, vd), *c_st.at())
            qlast[q] = (last_w1_pe, c_pe.at())
            if q == 0:
                guards["b2row_pe"] = c_pe.at()
            if q + 2 < 4:
                g = [(c_pe, qlast[q][1])]
                wq_t[q + 2] = load_quarter(W1_h[L], W2_h[L], q + 2, q % 2, g, g)
        buf_set("at", (c_dv, c_dv.at()), *c_st.at())
        guards["b1cc_ac"] = c_ac.at()
        return qlast[2][1], qlast[3][1]

    def emit_layer(L, wts, w1_guard, w2_guard):
        set_mm_pool([mmA, mmB])
        zero_w = [(c_dv, lin_state["rec_read"].get(30, 0)),
                  (c_dv, lin_state["state_tgt"]),
                  (c_dv, lin_state["kcum_read"]),
                  (c_dv, lin_state["rec_tgt"].get(31, 0))]
        E("gpsimd", lambda e: e.memset(ctx_state[:, :], 0.0), waits=zero_w, inc=c_gp)
        E("gpsimd", lambda e: e.memset(krun[:, :], 0.0), inc=c_gp)
        lin_state["state_zero"] = c_gp.at()
        vr = prologue_finish(L, 0, prologue_prime(L, 0))
        qkv_cur = emit_qkv(L, 0, vr, wts)
        prime_next = prologue_prime(L, 1)
        if dbg_q is not None and L == 0:
            E("sync", lambda e: e.dma_start(out=dbg_q, in_=qDloc[:, :]),
              waits=[(c_ac, qkv_cur["qDloc"])], inc=c_st)
            E("sync", lambda e: e.dma_start(out=dbg_ql, in_=qDlin[:, :]),
              waits=[(c_dv, qkv_cur["qDlin"])], inc=c_st)
            for i4 in range(4):
                E("sync", lambda e, i4=i4: e.dma_start(
                    out=dbg_kl[:, i4 * 512:(i4 + 1) * 512], in_=klin[i4][:, :]),
                  waits=[(c_dv, qkv_cur["klin"])], inc=c_st)
                E("sync", lambda e, i4=i4: e.dma_start(
                    out=dbg_v[:, i4 * D:(i4 + 1) * D], in_=vtok[i4][:, :]),
                  waits=[(c_ac, qkv_cur["v"])], inc=c_st)
                raddl(f"klin{i4}", c_st.at())
                raddl(f"v{i4}", c_st.at())
            E("sync", lambda e: e.dma_start(out=dbg_k, in_=kDloc[:, :]),
              waits=[(c_ac, qkv_cur["kDloc"])], inc=c_st)
            raddl("qDloc", c_st.at())
            raddl("kDloc", c_st.at())
            raddl("qDlin", c_st.at())
        ut_w0 = rtake("ut")
        rclear("ut")
        stepper_cur = AttnStepper(L, 0, qkv_cur, None, ut_w0)

        def make_pump(st):
            if st.ci < 16:
                return st.step
            last = [c_pe.at()]

            def pump():
                if c_pe.at() > last[0]:
                    if st.step_bucket():
                        last[0] = c_pe.at()
            return pump

        for p in range(NPANEL):
            qkv_next = None
            if p < NPANEL - 1:
                vr2 = prologue_finish(L, p + 1, prime_next)
                qkv_next = emit_qkv(L, p + 1, vr2, wts,
                                    interleave=make_pump(stepper_cur))
                if p < NPANEL - 2:
                    prime_next = prologue_prime(L, p + 2)
            stepper_cur.drain()
            ao_w = rtake("aoT")
            emit_attnT_out(L, p, stepper_cur, ao_w)
            ao_done = [(c_ac, c_ac.at()), (c_gp, c_gp.at())]
            stepper_next = None
            if p < NPANEL - 1:
                ut_wn = rtake("ut")
                rclear("ut")
                stepper_next = AttnStepper(L, p + 1, qkv_next, None, ut_wn)
            emit_wo(L, p, ao_done, wts[3],
                    interleave=(stepper_next.step_chain if stepper_next else None))
            if p == NPANEL - 1:
                guards["borow_pe"] = c_pe.at()
            if dbg_ao is not None and L == 0:
                E("sync", lambda e, p=p: e.dma_start(
                    out=dbg_ao[:, :, p * 512:(p + 1) * 512],
                    in_=aoT[:, :].rearrange("q (c n) -> q c n", c=8)),
                  waits=[(c_ac, c_ac.at()), (c_gp, c_gp.at())], inc=c_st)
                raddl("aoT", c_st.at())
            if qkv_next is not None:
                qkv_cur = qkv_next
                stepper_cur = stepper_next
        qkv_last = qkv_cur["last_pe"]
        return emit_mlp(L, [(c_pe, qkv_last)], [(c_pe, qkv_last), (c_pe, c_pe.at())])

    aff = load_layer_smalls(0)
    wts = (load_wchunks(SL_WQ, Wq_h[0], 0, D, 0, D),
           load_wchunks(SL_WK, Wk_h[0], 0, D, 0, D),
           load_wchunks(SL_WV, Wv_h[0], 0, D, 0, D),
           load_wchunks(SL_WO, Wo_h[0], 0, D, 0, D))

    def finish(last_tile_src=None):
        E("sync", lambda e: None, waits=c_st.at())
        b.finalize()
        return nc

    if stop_after == "prologue0":
        vr = emit_prologue(0, 0)
        for p in range(1, NPANEL):
            vr = emit_prologue(0, p)
        E("sync", lambda e: e.dma_start(out=out_h[0:128, :], in_=at[0][:, :]),
          waits=[(c_dv, c_dv.at())], inc=c_st)
        return finish()

    w1g, w2g = None, None
    nlayer = depth
    for L in range(nlayer):
        if L > 0:
            aff = load_layer_smalls(L)
            wts = (load_wchunks(SL_WQ, Wq_h[L], 0, D, 0, D, waits=[(c_pe, w1g)]),
                   load_wchunks(SL_WK, Wk_h[L], 0, D, 0, D, waits=[(c_pe, w1g)]),
                   load_wchunks(SL_WV, Wv_h[L], 0, D, 0, D, waits=[(c_pe, w2g)]),
                   load_wchunks(SL_WO, Wo_h[L], 0, D, 0, D, waits=[(c_pe, w2g)]))
        _r = emit_layer(L, wts, None, None)
        if _r == "STOP" or (isinstance(_r, tuple) and len(_r) == 2 and _r[0] == "STOP"):
            return finish()
        w1g, w2g = _r
        if stop_after == f"layer{L}":
            E("sync", lambda e: e.dma_start(out=out_h[0:128, :], in_=at[0][:, :]),
              waits=[(c_dv, c_dv.at())], inc=c_st)
            return finish()

    if stop_after == "attnmod":
        E("sync", lambda e: e.dma_start(out=out_h[0:128, :], in_=at[0][:, :]),
          waits=[(c_dv, c_dv.at())], inc=c_st)
        return finish()

    def xnt_chunk(c):
        if c < 4:
            return kDloc[:, c * 2048:(c + 1) * 2048]
        if c < 6:
            return h1T[:, (c - 4) * 2048:(c - 3) * 2048]
        return aoT[:, (c - 6) * 2048:(c - 5) * 2048]

    xnt_busy = (rtake("kDloc") + rtake("h1T") + rtake("aoT") +
                 [(c_pe, c_pe.at())] + c_st.at())
    xnt_ready = []
    for g4 in range(4):
        for i in range(4):
            t = g4 * 4 + i
            E("sync", lambda e, t=t, i=i: e.dma_start(
                out=hraw[i][:, :], in_=xn_hbm[t * 128:(t + 1) * 128, :]),
              waits=list(buf_wait("hraw")) + list(buf_wait(f"xn_hbm_t{t}")), inc=c_ldx)
        lxg = c_ldx.at()
        vr = do_transposes(lambda i, c: hraw[i][:, c * 128:(c + 1) * 128],
                           [lxg],
                           lambda c, g4=g4: xnt_chunk(c)[:, g4 * 512:(g4 + 1) * 512],
                           None, dst_busy=xnt_busy)
        xnt_busy = ()
        buf_set("hraw", *vr)
        xnt_ready = vr
    xnt_ld_w = xnt_ready

    E("gpsimd", lambda e: e.dma_start(
        out=gw_sb[:, :].rearrange("p (c n) -> p c n", n=4),
        in_=gate_w_h.rearrange("(c p) n -> p c n", p=128)), inc=c_ldw)
    E("gpsimd", lambda e: e.dma_start(out=gb_row[:, :], in_=gate_b_h), inc=c_ldw)
    gw_ld = c_ldw.at()

    for tt in range(NT):
        pairs = [(xnt_chunk(c)[:, tt * 128:(tt + 1) * 128], gw_sb[:, c * 4:(c + 1) * 4])
                 for c in range(8)]
        rank1 = (ones_bf[0:1, 0:128], gb_row[0:1, 0:4])
        pe = mm_group(spare[:, 0:4], pairs,
                      waits=xnt_ld_w + [gw_ld] + bfree(spare), rank1=rank1)
        col = tt % 16
        E("vector", lambda e, col=col: e.reduce_max(
            sm_m[:, col:col + 1], spare[:, 0:4], axis=AX, negate=True),
          waits=[(c_pe, pe)] + sm_guard.get(col, []), inc=c_dv)
        v_m = c_dv.at()
        E("scalar", lambda e, col=col: e.activation(
            gsc[:, 0:4], spare[:, 0:4], AF.Exp,
            bias=sm_m[:, col:col + 1], accum_out=sm_s[:, col:col + 1]),
          waits=[(c_dv, v_m), (c_dv, c_dv.at())], inc=c_ac)
        a_e = c_ac.at()
        bset(spare, (c_ac, a_e))
        sm_guard[col] = [(c_ac, a_e)]
        E("vector", lambda e, col=col: e.reciprocal(sm_r[:, col:col + 1], sm_s[:, col:col + 1]),
          waits=[(c_ac, a_e)], inc=c_dv)
        v_r = c_dv.at()
        E("vector", lambda e, tt=tt, col=col: e.tensor_scalar_mul(
            gwt[:, tt * 4:(tt + 1) * 4], gsc[:, 0:4], sm_r[:, col:col + 1]),
          waits=[(c_dv, v_r)], inc=c_dv)
        g0 = gwt[:, tt * 4 + 0:tt * 4 + 1]
        g1 = gwt[:, tt * 4 + 1:tt * 4 + 2]
        g2 = gwt[:, tt * 4 + 2:tt * 4 + 3]
        g3 = gwt[:, tt * 4 + 3:tt * 4 + 4]
        def step(fn):
            E("vector", fn, waits=[(c_dv, c_dv.at())], inc=c_dv)
        step(lambda e, g0=g0, g1=g1: e.tensor_tensor(gsc[:, 0:1], g0, g1, ALU.max))
        step(lambda e, g2=g2, g3=g3: e.tensor_tensor(gsc[:, 1:2], g2, g3, ALU.max))
        step(lambda e, g0=g0, g1=g1: e.tensor_tensor(gsc[:, 2:3], g0, g1, ALU.min))
        step(lambda e, g2=g2, g3=g3: e.tensor_tensor(gsc[:, 3:4], g2, g3, ALU.min))
        step(lambda e: e.tensor_tensor(gsc[:, 4:5], gsc[:, 0:1], gsc[:, 1:2], ALU.min))
        step(lambda e: e.tensor_tensor(gsc[:, 5:6], gsc[:, 2:3], gsc[:, 3:4], ALU.max))
        step(lambda e: e.tensor_tensor(gsc[:, 6:7], gsc[:, 4:5], gsc[:, 5:6], ALU.max))
        step(lambda e, tt=tt: e.tensor_scalar(
            gsc[:, 0:4], gwt[:, tt * 4:(tt + 1) * 4], gsc[:, 6:7], None, ALU.is_ge))
        step(lambda e, tt=tt: e.tensor_tensor(
            scg[:, tt * 4:(tt + 1) * 4], gsc[:, 0:4], gwt[:, tt * 4:(tt + 1) * 4],
            ALU.mult))
    gate_done = c_dv.at()
    if "gate" in taps:
        dbg_gate = dram("dbg_gate", [128, 128], F32, kind="ExternalOutput")
        E("sync", lambda e: e.dma_start(out=dbg_gate[:, 0:64], in_=gwt[:, :]),
          waits=[(c_dv, gate_done)], inc=c_st)
        E("sync", lambda e: e.dma_start(out=dbg_gate[:, 64:128], in_=scg[:, :]),
          waits=[(c_dv, gate_done)], inc=c_st)

    if stop_after == "gate":
        E("sync", lambda e: e.dma_start(out=out_h[0:128, :], in_=at[0][:, :]),
          waits=[(c_dv, c_dv.at())], inc=c_st)
        return finish()

    set_mm_pool([mmA, mmB, dA, dB, kcum])
    experts = [("s", 0), ("s", 1), ("r", 0), ("r", 1), ("r", 2), ("r", 3)]
    acc_guard = {}
    b1cc_par_read = [c_ac.at(), 0]
    qstream = [(ei, q) for ei in range(6) for q in range(4)]

    def exp_srcs(ei):
        kind, ex = experts[ei]
        return (sw1_h[ex] if kind == "s" else rw1_h[ex],
                sw2_h[ex] if kind == "s" else rw2_h[ex])

    eq_t = {}

    def load_eq(gq, waits1, waits2):
        ei, q = qstream[gq]
        w1src, w2src = exp_srcs(ei)
        eq_t[gq] = load_quarter(w1src, w2src, q, gq % 2, waits1, waits2)
    load_eq(0, [(c_pe, w1g)], [(c_pe, w1g)])
    load_eq(1, [(c_pe, w2g)], [(c_pe, w2g)])
    bias_ok_map = {}

    def emit_expert_bias(ei):
        kind, ex = experts[ei]
        boff = (ei % 2) * 32
        b1src = sb1_h[ex:ex + 1, :] if kind == "s" else rb1_h[ex:ex + 1, :]
        b2src = sb2_h[ex:ex + 1, :] if kind == "s" else rb2_h[ex:ex + 1, :]
        E("gpsimd", lambda e, b1src=b1src: e.dma_start(out=b1row[:, :], in_=b1src),
          waits=[(c_pe, guards["rows_pe"])], inc=c_ldw)
        E("gpsimd", lambda e, b2src=b2src: e.dma_start(out=b2row[:, :], in_=b2src),
          waits=[(c_pe, guards["b2row_pe"])], inc=c_ldw)
        wb_t = c_ldw.at()

        def tr_b1(e):
            ins = None
            for c in range(32):
                ins = e.matmul(spare[:, 32 + c:33 + c], b1row[0:1, c * 128:(c + 1) * 128],
                               ones_bf[0:1, 0:1], start=(c == 0), stop=(c == 31))
            return ins
        E("tensor", tr_b1, waits=[wb_t, (c_ac, b1cc_par_read[ei % 2])] + bfree(spare),
          inc=c_pe)
        pe_b = c_pe.at()
        guards["rows_pe"] = pe_b
        E("vector", lambda e, boff=boff: e.tensor_copy(
            b1cc[:, boff:boff + 32], spare[:, 32:64]),
          waits=[(c_pe, pe_b)], inc=c_dv)
        bset(spare, (c_dv, c_dv.at()))
        bias_ok_map[ei] = c_dv.at()
    emit_expert_bias(0)
    for gq, (ei, q) in enumerate(qstream):
        kind, ex = experts[ei]
        boff = (ei % 2) * 32
        bias_ok = bias_ok_map[ei]
        b1, b2 = SLQ[gq % 2]
        w1_t, w2_t = eq_t[gq]
        for sp in range(8):
            init = (gq == 0)
            at_pre = {}
            for ii in (0, 1):
                t = sp * 2 + ii
                atb = at[2 * (sp % 2) + ii]
                if not init:
                    E("sync", lambda e, t=t, atb=atb: e.dma_start(
                        out=atb[:, :], in_=acc_hbm[t * 128:(t + 1) * 128, :]),
                      waits=[acc_guard.get(t, None), buf_wait("at"),
                             buf_wait(f"atacc{2 * (sp % 2) + ii}")],
                      inc=c_ldx)
                at_pre[ii] = c_ldx.at()
            ut_w = rtake("ut")
            rclear("ut")
            for fc in range(8):
                bank = next_mm()
                pairs = [(arena[:, b1 + c * 1024 + fc * 128: b1 + c * 1024 + (fc + 1) * 128],
                          xnt_chunk(c)[:, sp * 256:(sp + 1) * 256]) for c in range(8)]
                pe = mm_group(bank[:, 0:256], pairs,
                              waits=xnt_ld_w + [w1_t] + bfree(bank))
                bcol = b1cc[:, boff + q * 8 + fc: boff + q * 8 + fc + 1]
                dst = ut[:, fc * 256:(fc + 1) * 256]
                E("scalar", lambda e, bank=bank, dst=dst, bcol=bcol: e.activation(
                    dst, bank[:, 0:256], AF.Relu, bias=bcol),
                  waits=[(c_pe, pe), (c_dv, bias_ok)] + ut_w, inc=c_ac)
                bset(bank, (c_ac, c_ac.at()))
            ut_t = c_ac.at()
            b1cc_par_read[ei % 2] = ut_t
            for ii in (0, 1):
                t = sp * 2 + ii
                atb = at[2 * (sp % 2) + ii]
                lxa = at_pre[ii]
                for dh in (0, 1):
                    bank = next_mm()
                    pairsA = [(ut[:, fc * 256 + ii * 128: fc * 256 + ii * 128 + 128],
                               arena[:, b2 + fc * 1024 + dh * 512: b2 + fc * 1024 + (dh + 1) * 512])
                              for fc in range(8)]
                    rank1 = (ones_bf[0:1, 0:128], b2row[0:1, dh * 512:(dh + 1) * 512]) \
                        if q == 0 else None
                    pe = mm_group(bank[:, :], pairsA,
                                  waits=[(c_ac, ut_t), w2_t] + bfree(bank),
                                  rank1=rank1)
                    radd("ut", c_pe, pe)
                    if kind == "s":
                        if init:
                            E("vector", lambda e, bank=bank, atb=atb, dh=dh: e.tensor_scalar_mul(
                                atb[:, dh * 512:(dh + 1) * 512], bank[:, :], 0.5),
                              waits=[(c_pe, pe), buf_wait("at"),
                                     buf_wait(f"atacc{2 * (sp % 2) + ii}")],
                              inc=c_dv)
                        else:
                            E("vector", lambda e, bank=bank, atb=atb, dh=dh: e.scalar_tensor_tensor(
                                atb[:, dh * 512:(dh + 1) * 512], bank[:, :], 0.5,
                                atb[:, dh * 512:(dh + 1) * 512], ALU.mult, ALU.add),
                              waits=[(c_pe, pe), lxa], inc=c_dv)
                    else:
                        sc_col = scg[:, t * 4 + ex: t * 4 + ex + 1]
                        E("vector", lambda e, bank=bank, atb=atb, dh=dh, sc_col=sc_col:
                          e.scalar_tensor_tensor(
                              atb[:, dh * 512:(dh + 1) * 512], bank[:, :], sc_col,
                              atb[:, dh * 512:(dh + 1) * 512], ALU.mult, ALU.add),
                          waits=[(c_pe, pe), lxa, (c_dv, gate_done)], inc=c_dv)
                    bset(bank, (c_dv, c_dv.at()))
                vd = c_dv.at()
                E("sync", lambda e, t=t, atb=atb: e.dma_start(
                    out=acc_hbm[t * 128:(t + 1) * 128, :], in_=atb[:, :]),
                  waits=[(c_dv, vd)], inc=c_st)
                acc_guard[t] = c_st.at()
                buf_set(f"atacc{2 * (sp % 2) + ii}", (c_dv, vd), *c_st.at())
        if q == 0:
            guards["b2row_pe"] = c_pe.at()
            if ei + 1 < len(experts):
                emit_expert_bias(ei + 1)
        if gq + 2 < len(qstream):
            g = [(c_pe, c_pe.at())]
            load_eq(gq + 2, g, g)

    if stop_after == "ffn":
        E("sync", lambda e: e.dma_start(out=out_h[0:128, :], in_=at[0][:, :]),
          waits=[(c_dv, c_dv.at())], inc=c_st)
        return finish()

    comb_prev = list(buf_wait("at"))
    for k in range(4):
        comb_prev += list(buf_wait(f"atacc{k}"))
    for tt in range(NT):
        E("sync", lambda e, tt=tt: e.dma_start(
            out=at[0][:, :], in_=acc_hbm[tt * 128:(tt + 1) * 128, :]),
          waits=[acc_guard.get(tt, None)] + comb_prev, inc=c_ldx)
        E("sync", lambda e, tt=tt: e.dma_start(
            out=at[2][:, :], in_=x_h[tt * 128:(tt + 1) * 128, :]), inc=c_ldx)
        E("sync", lambda e, tt=tt: e.dma_start(
            out=at[3][:, :], in_=a_hbm[tt * 128:(tt + 1) * 128, :]),
          waits=[buf_wait(f"a_hbm_t{tt}")], inc=c_ldx)
        lx = c_ldx.at()
        E("scalar", lambda e: e.activation(at[1][:, :], at[0][:, :], AF.Silu),
          waits=[lx], inc=c_ac)
        a_s = c_ac.at()
        E("vector", lambda e: e.tensor_mul(at[0][:, :], at[1][:, :], at[3][:, :]),
          waits=[(c_ac, a_s)], inc=c_dv)
        v_p = c_dv.at()
        E("vector", lambda e: e.tensor_add(at[1][:, :], at[0][:, :], at[2][:, :]),
          waits=[(c_dv, v_p)], inc=c_dv)
        v_c = c_dv.at()
        E("sync", lambda e, tt=tt: e.dma_start(
            out=out_h[tt * 128:(tt + 1) * 128, :], in_=at[1][:, :]),
          waits=[(c_dv, v_c)], inc=c_st)
        comb_prev = [(c_dv, v_c)] + c_st.at()

    return finish()


def make_consts():
    ident = np.eye(128, dtype=np.float32).astype(ml_dtypes.bfloat16)
    m = np.zeros((128, 512), np.float32)
    causal = np.triu(np.full((128, 128), NEG, np.float32), 1)
    m[:, 0:128] = NEG
    m[:, 128:256] = causal
    m[:, 256:384] = 0.0
    m[:, 384:512] = causal
    return ident, m.astype(ml_dtypes.bfloat16)


def make_in_maps(inputs):
    ident, masks = make_consts()
    x = np.ascontiguousarray(np.asarray(inputs["L0_attn_x"], dtype=np.float32))
    f32c = lambda v: np.ascontiguousarray(np.asarray(v, dtype=np.float32))
    common = {
        "Wq": f32c(inputs["L0_attn_Wq"]), "Wk": f32c(inputs["L0_attn_Wk"]),
        "Wv": f32c(inputs["L0_attn_Wv"]), "Wo": f32c(inputs["L0_attn_Wo"]),
        "bo": f32c(inputs["L0_attn_bo"]),
        "ln1_g": f32c(inputs["L0_attn_ln1_g"]), "ln1_b": f32c(inputs["L0_attn_ln1_b"]),
        "W1": f32c(inputs["L0_attn_W1"]), "b1": f32c(inputs["L0_attn_b1"]),
        "W2": f32c(inputs["L0_attn_W2"]), "b2": f32c(inputs["L0_attn_b2"]),
        "ln2_g": f32c(inputs["L0_attn_ln2_g"]), "ln2_b": f32c(inputs["L0_attn_ln2_b"]),
        "gate_w": f32c(inputs["L0_ffn_gate_w"]),
        "gate_b": f32c(inputs["L0_ffn_gate_b"]).reshape(1, 4),
        "sw1": f32c(inputs["L0_ffn_sw1"]), "sb1": f32c(inputs["L0_ffn_sb1"]),
        "sw2": f32c(inputs["L0_ffn_sw2"]), "sb2": f32c(inputs["L0_ffn_sb2"]),
        "rw1": f32c(inputs["L0_ffn_rw1"]), "rb1": f32c(inputs["L0_ffn_rb1"]),
        "rw2": f32c(inputs["L0_ffn_rw2"]), "rb2": f32c(inputs["L0_ffn_rb2"]),
        "ident": ident, "masks": masks,
    }
    in_maps = []
    for core in range(8):
        m = dict(common)
        m["x"] = np.ascontiguousarray(x[core])
        in_maps.append(m)
    return in_maps


def kernel(**inputs):
    nc = build_program()
    in_maps = make_in_maps(inputs)
    res = run_bass_kernel_spmd(nc, in_maps, core_ids=list(range(8)))
    return np.stack([r["out"] for r in res.results], axis=0)
